# version 22
# baseline (speedup 1.0000x reference)
import numpy as np
from contextlib import ExitStack

import concourse.bass as bass
import concourse.mybir as mybir
import concourse.bass_utils as _bass_utils
from concourse.bass_utils import run_bass_kernel_spmd

if not getattr(_bass_utils, "_ldwopt_patched", False):
    _orig_run_command = _bass_utils.run_command

    def _run_command_ldwopt(cmd, *a, **kw):
        if isinstance(cmd, list):
            cmd = ["--enable-ldw-opt=true" if c == "--enable-ldw-opt=false" else c for c in cmd]
        return _orig_run_command(cmd, *a, **kw)

    _bass_utils.run_command = _run_command_ldwopt
    _bass_utils._ldwopt_patched = True

F32 = mybir.dt.float32
BF16 = mybir.dt.bfloat16
I32 = mybir.dt.int32
AF = mybir.ActivationFunctionType
ALU = mybir.AluOpType
AX = mybir.AxisListType

T = 2048
E = 1024
H = 16
D = 64
DH = 4096
NSEQ = 2
NT = T // 128
NEC = E // 128
NCC = DH // 128
SC = 1.0 / 32.0
NEG = -10000.0


class Ctr:
    def __init__(self, stk, nc, name):
        self.sem = stk.enter_context(nc.semaphore(name))
        self.n = 0

    def inc(self, instr, k=1):
        instr.then_inc(self.sem, k)
        self.n += k
        return instr

    def wait(self, eng, n=None):
        eng.wait_ge(self.sem, self.n if n is None else n)


def build_nc():
    nc = bass.Bass("TRN2", target_bir_lowering=False)

    dr = lambda name, shape, dt=F32, kind="Internal": nc.dram_tensor(name, shape, dt, kind=kind)

    x_in = dr("x", [NSEQ, T, E], kind="ExternalInput")
    wq_in = dr("wq", [E, E], kind="ExternalInput")
    wk_in = dr("wk", [E, E], kind="ExternalInput")
    wv_in = dr("wv", [E, E], kind="ExternalInput")
    wo_in = dr("wo", [E, E], kind="ExternalInput")
    cb_in = dr("cbT2", [128, H], kind="ExternalInput")
    rb_in = dr("rbT2", [128, H], kind="ExternalInput")
    wrel_in = dr("wrel2", [128, D], kind="ExternalInput")
    brel_in = dr("brel2", [128, 1], kind="ExternalInput")
    bo_in = dr("bo", [1, E], kind="ExternalInput")
    w1t_in = dr("w1t", [E, DH], kind="ExternalInput")
    w2t_in = dr("w2t", [DH, E], kind="ExternalInput")
    convw_in = dr("convw", [128, NCC * 3], kind="ExternalInput")
    convb_in = dr("convb", [128, NCC], kind="ExternalInput")
    enc_in = dr("encsel2", [128, 257], kind="ExternalInput")
    ident_in = dr("ident", [128, 128], kind="ExternalInput")
    edge0_in = dr("edge0", [128, 257], kind="ExternalInput")
    edge15_in = dr("edge15", [128, 257], kind="ExternalInput")
    out_dr = dr("out", [NSEQ, T, E], kind="ExternalOutput")

    q_dram = dr("q_s", [128, NEC, T])
    k_dram = dr("k_s", [128, NEC, T])
    v_dram = dr("v_s", [T, E])
    xq_dram = dr("xq_s", [128, NEC, T], BF16)
    x2q_dram = dr("x2q_s", [128, NCC, T], BF16)
    y2_dram = dr("y2_s", [T, E])
    rho_dram = dr("rho_s", [H, T, 512])
    w1q_dram = dr("w1q_s", [128, NEC, DH], BF16)
    w2q_dram = dr("w2q_s", [128, NCC, E], BF16)

    stk = ExitStack()
    sb = lambda name, shape, dt=F32: stk.enter_context(nc.sbuf_tensor(name, shape, dt))

    ident = sb("identt", [128, 128])
    cb_sb = sb("cb", [128, H])
    rb_sb = sb("rb", [128, H])
    wrel_sb = sb("wrelb", [128, D])
    brel_sb = sb("brelb", [128, 1])
    enc_sb = sb("encb", [128, 257])
    edge0_sb = sb("edge0b", [128, 257])
    edge15_sb = sb("edge15b", [128, 257])
    convb_sb = sb("convbb", [128, NCC])
    convw_sb = sb("convwb", [128, NCC * 3])
    pt_sb = sb("ptband", [128, 257])
    bo_b = sb("bo_b", [128, E])
    ones_sb = sb("ones", [1, 128])
    sca = sb("sca", [128, 8])
    bo_row = sb("bo_row", [1, E])
    zero_sb = sb("zeroc", [128, 1])
    eps_sb = sb("epsc", [128, 1])
    barv = sb("barv", [1, 8])

    PS = stk.enter_context(nc.psum_tensor("PS", [128, 4096], F32))

    c_ld = Ctr(stk, nc, "c_ld")
    c_wr = Ctr(stk, nc, "c_wr")
    c_rd = Ctr(stk, nc, "c_rd")
    c_te = Ctr(stk, nc, "c_te")
    c_ve = Ctr(stk, nc, "c_ve")
    c_se = Ctr(stk, nc, "c_se")
    c_out = Ctr(stk, nc, "c_out")
    c_bar = Ctr(stk, nc, "c_bar")
    c_x = [Ctr(stk, nc, "c_x0"), Ctr(stk, nc, "c_x1")]
    c_f = [Ctr(stk, nc, "c_f0"), Ctr(stk, nc, "c_f1")]
    c_g = [Ctr(stk, nc, "c_g0"), Ctr(stk, nc, "c_g1")]
    c_qr = [Ctr(stk, nc, "c_qr0"), Ctr(stk, nc, "c_qr1")]
    c_kv = Ctr(stk, nc, "c_kv")
    c_skw2 = [Ctr(stk, nc, f"c_skw{i}") for i in range(2)]
    c_mrg = Ctr(stk, nc, "c_mrg")
    c_gp = Ctr(stk, nc, "c_gp")
    c_rho = Ctr(stk, nc, "c_rho")

    block = stk.enter_context(nc.Block())

    def vs(vector, instr):
        c_ve.inc(instr)
        vector.wait_ge(c_ve.sem, c_ve.n)

    ENGS = None

    def barrier():
        c_bar.n += 5
        tgt = c_bar.n

        @block.sync
        def _(eng, tgt=tgt):
            for ctr in (c_ld, c_wr, c_rd, c_out, c_kv, c_skw2[0], c_skw2[1], c_mrg, c_rho,
                        c_x[0], c_x[1], c_f[0], c_f[1], c_g[0], c_g[1],
                        c_qr[0], c_qr[1]):
                if ctr.n:
                    eng.wait_ge(ctr.sem, ctr.n)
            eng.nop().then_inc(c_bar.sem, 1)
            eng.wait_ge(c_bar.sem, tgt)

        @block.vector
        def _(eng, tgt=tgt):
            eng.memset(barv[0:1, 0:1], 0.0).then_inc(c_bar.sem, 1)
            eng.wait_ge(c_bar.sem, tgt)

        @block.scalar
        def _(eng, tgt=tgt):
            eng.activation(out=barv[0:1, 1:2], in_=zero_sb[0:1, 0:1], func=AF.Relu,
                           bias=zero_sb[0:1, 0:1]).then_inc(c_bar.sem, 1)
            eng.wait_ge(c_bar.sem, tgt)

        @block.tensor
        def _(eng, tgt=tgt):
            eng.wait_ge(c_ve.sem, c_ve.n)
            eng.wait_ge(c_se.sem, c_se.n)
            eng.matmul(PS[0:1, 4095:4096], ones_sb[0:1, 0:1], ones_sb[0:1, 0:1],
                       start=True, stop=True).then_inc(c_bar.sem, 1)
            eng.wait_ge(c_bar.sem, tgt)

        @block.gpsimd
        def _(eng, tgt=tgt):
            eng.memset(barv[0:1, 3:4], 0.0).then_inc(c_bar.sem, 1)
            eng.wait_ge(c_bar.sem, tgt)

    @block.sync
    def _(sync):
        for dst, src in [(ident, ident_in), (cb_sb, cb_in), (rb_sb, rb_in),
                         (wrel_sb, wrel_in), (brel_sb, brel_in), (enc_sb, enc_in),
                         (edge0_sb, edge0_in), (edge15_sb, edge15_in),
                         (convb_sb, convb_in), (convw_sb, convw_in), (bo_row, bo_in)]:
            c_ld.inc(sync.dma_start(out=dst[:], in_=src.ap()), 16)

    @block.vector
    def _(vector):
        vector.memset(ones_sb[:], 1.0)
        vector.memset(zero_sb[:], 0.0)
        vector.memset(eps_sb[:], 1e-5).then_inc(c_ve.sem, 1)
    c_ve.n += 1

    @block.scalar
    def _(scalar):
        scalar.wait_ge(c_ve.sem, 1)

    @block.tensor
    def _(tensor):
        c_ld.wait(tensor)
        c_ve.wait(tensor)
        tensor.matmul(PS[0:64, 0:257], wrel_sb[0:64, :], enc_sb[0:64, :], start=True, stop=True)
        tensor.matmul(PS[:, 512:1024], ones_sb[0:1, :], bo_row[0:1, 0:512], start=True, stop=True)
        m = tensor.matmul(PS[:, 1024:1536], ones_sb[0:1, :], bo_row[0:1, 512:1024], start=True, stop=True)
        c_te.inc(m)

    @block.vector
    def _(vector):
        c_te.wait(vector)
        vector.tensor_scalar(out=pt_sb[0:64, :], in0=PS[0:64, 0:257], scalar1=brel_sb[0:64, 0:1],
                             scalar2=None, op0=ALU.add)
        vector.tensor_copy(out=bo_b[:, 0:512], in_=PS[:, 512:1024])
        v = vector.tensor_copy(out=bo_b[:, 512:1024], in_=PS[:, 1024:1536])
        c_ve.inc(v)

    @block.gpsimd
    def _(gpsimd):
        c_ve.wait(gpsimd)
        c_rd.inc(gpsimd.dma_start(out=pt_sb[64:128, :], in_=pt_sb[0:64, :]), 16)

    barrier()

    with ExitStack() as p1:
        wbuf = p1.enter_context(nc.sbuf_tensor("wbuf", [128, 4096], F32))
        wqb = p1.enter_context(nc.sbuf_tensor("wqb", [128, 4096], BF16))
        dbuf = p1.enter_context(nc.sbuf_tensor("dbuf", [128, 4096], F32))
        ebuf = p1.enter_context(nc.sbuf_tensor("ebuf", [128, 4096], F32))
        acc = p1.enter_context(nc.sbuf_tensor("acc", [128, 2], F32))
        tiny = p1.enter_context(nc.sbuf_tensor("tiny", [128, 8], F32))

        for wi, (w_in, wq_dr, nch, ncols, scol, mcol, nelem) in enumerate([
                (w1t_in, w1q_dram, NEC, DH, 0, 1, E * DH),
                (w2t_in, w2q_dram, NCC, E, 2, 3, DH * E)]):
            @block.sync
            def _(sync, w_in=w_in, nch=nch, ncols=ncols, vbase=c_ve.n):
                for c in range(nch):
                    c_ld.inc(sync.dma_start(out=wbuf[:, 0:ncols], in_=w_in.ap()[c * 128:(c + 1) * 128, :]), 16)
                    sync.wait_ge(c_ve.sem, vbase + 2 * c + 1)
            v0 = c_ve.n

            @block.vector
            def _(vector, nch=nch, ncols=ncols, base=c_ld.n - 16 * nch):
                for c in range(nch):
                    vector.wait_ge(c_ld.sem, base + 16 * (c + 1))
                    vs(vector, vector.tensor_reduce(out=acc[:, 1:2], in_=wbuf[:, 0:ncols], axis=AX.X,
                                                    op=ALU.add, apply_absolute_value=True))
                    if c == 0:
                        vs(vector, vector.tensor_copy(out=acc[:, 0:1], in_=acc[:, 1:2]))
                    else:
                        vs(vector, vector.tensor_add(out=acc[:, 0:1], in0=acc[:, 0:1], in1=acc[:, 1:2]))

            @block.tensor
            def _(tensor):
                c_ve.wait(tensor)
                c_te.inc(tensor.transpose(PS[0:1, 0:128], acc[:, 0:1], ident[:]))

            @block.vector
            def _(vector, scol=scol, mcol=mcol, nelem=nelem):
                c_te.wait(vector)
                vs(vector, vector.reduce_sum(out=tiny[0:1, 0:1], in_=PS[0:1, 0:128], axis=AX.X))
                vs(vector, vector.tensor_scalar(out=tiny[0:1, 1:2], in0=tiny[0:1, 0:1], scalar1=1.0 / nelem,
                                                scalar2=1e-5, op0=ALU.mult, op1=ALU.max))
                vs(vector, vector.reciprocal(out=tiny[0:1, 2:3], in_=tiny[0:1, 1:2]))
                v = vector.tensor_scalar(out=tiny[0:1, 3:4], in0=tiny[0:1, 1:2], scalar1=1.0 / 127.0,
                                         scalar2=None, op0=ALU.mult)
                c_ve.inc(v)

            @block.tensor
            def _(tensor, scol=scol):
                c_ve.wait(tensor)
                tensor.matmul(PS[:, 256:257], ones_sb[0:1, :], tiny[0:1, 2:3], start=True, stop=True)
                m = tensor.matmul(PS[:, 257:258], ones_sb[0:1, :], tiny[0:1, 3:4], start=True, stop=True)
                c_te.inc(m)

            @block.vector
            def _(vector, scol=scol, mcol=mcol):
                c_te.wait(vector)
                vs(vector, vector.tensor_copy(out=sca[:, scol:scol + 1], in_=PS[:, 256:257]))
                vs(vector, vector.tensor_copy(out=sca[:, mcol:mcol + 1], in_=PS[:, 257:258]))

            @block.sync
            def _(sync, w_in=w_in, nch=nch, ncols=ncols, vbase2=c_ve.n):
                for c in range(nch):
                    sync.wait_ge(c_wr.sem, c_wr.n + 16 * c)
                    c_ld.inc(sync.dma_start(out=wbuf[:, 0:ncols], in_=w_in.ap()[c * 128:(c + 1) * 128, :]), 16)
                    sync.wait_ge(c_ve.sem, vbase2 + 2 * (c + 1))

            @block.vector
            def _(vector, nch=nch, ncols=ncols, scol=scol, base=c_ld.n - 16 * nch, base_wr=c_wr.n):
                for c in range(nch):
                    vector.wait_ge(c_ld.sem, base + 16 * (c + 1))
                    if c > 0:
                        vector.wait_ge(c_wr.sem, base_wr + 16 * c)
                    vector.tensor_scalar(out=dbuf[:, 0:ncols], in0=wbuf[:, 0:ncols],
                                         scalar1=sca[:, scol:scol + 1], scalar2=0.5,
                                         op0=ALU.mult, op1=ALU.is_ge)
                    vs(vector, vector.tensor_scalar(out=ebuf[:, 0:ncols], in0=wbuf[:, 0:ncols],
                                                    scalar1=sca[:, scol:scol + 1], scalar2=-0.5,
                                                    op0=ALU.mult, op1=ALU.is_le))
                    v = vector.tensor_sub(out=wqb[:, 0:ncols], in0=ebuf[:, 0:ncols], in1=dbuf[:, 0:ncols])
                    c_ve.inc(v)

            @block.gpsimd
            def _(gpsimd, wq_dr=wq_dr, nch=nch, ncols=ncols, base=c_ve.n - 2 * nch):
                for c in range(nch):
                    gpsimd.wait_ge(c_ve.sem, base + 2 * (c + 1))
                    c_wr.inc(gpsimd.dma_start(out=wq_dr.ap()[:, c, :], in_=wqb[:, 0:ncols]), 16)
            barrier()

    for s in range(NSEQ):
        with ExitStack() as sq:
            cxbuf = sq.enter_context(nc.sbuf_tensor(f"cx{s}", [128, NT], F32))
            bbuf = sq.enter_context(nc.sbuf_tensor(f"bb{s}", [128, NT], F32))
            arow = sq.enter_context(nc.sbuf_tensor(f"ar{s}", [1, T + 2], F32))

            sub_ab = ExitStack()
            xn_fm = sub_ab.enter_context(nc.sbuf_tensor(f"xnfm{s}", [128, NEC * T], F32))

            with ExitStack() as ph:
                x_tok = [ph.enter_context(nc.sbuf_tensor(f"xt{s}{i}", [128, E], F32)) for i in range(2)]
                xn_tok = ph.enter_context(nc.sbuf_tensor(f"xnt{s}", [128, E], F32))
                xq_tok = ph.enter_context(nc.sbuf_tensor(f"xqt{s}", [128, E], F32))
                ci_t = ph.enter_context(nc.sbuf_tensor(f"ci{s}", [128, E], I32))
                d_t = ph.enter_context(nc.sbuf_tensor(f"d{s}", [128, E], F32))
                e_t = ph.enter_context(nc.sbuf_tensor(f"e{s}", [128, E], F32))
                sqd = ph.enter_context(nc.sbuf_tensor(f"sqd{s}", [128, E], F32))
                st_ = ph.enter_context(nc.sbuf_tensor(f"st{s}", [128, 8], F32))
                xqfm_sb = ph.enter_context(nc.sbuf_tensor(f"xqfm{s}", [128, NEC * 128], BF16))

                for i in range(NT):
                    a = i * 128
                    buf = x_tok[i % 2]

                    @block.sync
                    def _(sync, buf=buf, i=i, a=a):
                        if i >= 2:
                            sync.wait_ge(c_ve.sem, c_ve.n)
                        c_x[i % 2].inc(sync.dma_start(out=buf[:], in_=x_in.ap()[s, a:a + 128, :]), 16)

                    @block.vector
                    def _(vector, buf=buf, i=i):
                        c_x[i % 2].wait(vector)
                        vector.scalar_tensor_tensor(out=sqd[:], in0=buf[:], scalar=1.0, in1=buf[:],
                                                    op0=ALU.mult, op1=ALU.mult, accum_out=st_[:, 0:1])
                        vs(vector, vector.reduce_sum(out=st_[:, 1:2], in_=buf[:], axis=AX.X))
                        vs(vector, vector.scalar_tensor_tensor(out=st_[:, 2:3], in0=st_[:, 1:2], scalar=1.0 / E,
                                                               in1=st_[:, 1:2], op0=ALU.mult, op1=ALU.mult))
                        vector.tensor_sub(out=st_[:, 3:4], in0=st_[:, 0:1], in1=st_[:, 2:3])
                        v = vector.tensor_scalar(out=st_[:, 4:5], in0=st_[:, 1:2], scalar1=-1.0 / E,
                                                 scalar2=None, op0=ALU.mult)
                        c_ve.inc(v)

                    @block.scalar
                    def _(scalar):
                        c_ve.wait(scalar)
                        ss = scalar.activation(out=st_[:, 5:6], in_=st_[:, 3:4], func=AF.Sqrt,
                                               bias=eps_sb[:, 0:1], scale=1.0 / E)
                        c_se.inc(ss)

                    @block.vector
                    def _(vector, buf=buf, i=i):
                        c_se.wait(vector)
                        vs(vector, vector.reciprocal(out=st_[:, 6:7], in_=st_[:, 5:6]))
                        vs(vector, vector.tensor_scalar(out=xn_tok[:], in0=buf[:], scalar1=st_[:, 4:5],
                                                        scalar2=st_[:, 6:7], op0=ALU.add, op1=ALU.mult))
                        vs(vector, vector.reduce_max(out=st_[:, 7:8], in_=xn_tok[:], axis=AX.X,
                                                     apply_absolute_value=True))
                        vs(vector, vector.tensor_scalar(out=cxbuf[:, i:i + 1], in0=st_[:, 7:8], scalar1=1e-5,
                                                        scalar2=None, op0=ALU.max))
                        vs(vector, vector.reciprocal(out=st_[:, 7:8], in_=cxbuf[:, i:i + 1]))
                        vs(vector, vector.tensor_scalar(out=st_[:, 7:8], in0=st_[:, 7:8], scalar1=127.0,
                                                        scalar2=None, op0=ALU.mult))
                        vs(vector, vector.tensor_scalar(out=xq_tok[:], in0=xn_tok[:], scalar1=st_[:, 7:8],
                                                        scalar2=None, op0=ALU.mult))
                        vs(vector, vector.tensor_copy(out=ci_t[:], in_=xq_tok[:]))
                        vector.scalar_tensor_tensor(out=d_t[:], in0=ci_t[:], scalar=0.5, in1=xq_tok[:],
                                                    op0=ALU.add, op1=ALU.is_le)
                        vs(vector, vector.scalar_tensor_tensor(out=e_t[:], in0=ci_t[:], scalar=-0.5, in1=xq_tok[:],
                                                               op0=ALU.add, op1=ALU.is_ge))
                        vs(vector, vector.tensor_sub(out=e_t[:], in0=e_t[:], in1=d_t[:]))
                        v = vector.scalar_tensor_tensor(out=xq_tok[:], in0=ci_t[:], scalar=-1.0,
                                                        in1=e_t[:], op0=ALU.mult, op1=ALU.add)
                        c_ve.inc(v)

                    @block.tensor
                    def _(tensor):
                        c_ve.wait(tensor)
                        for j in range(NEC):
                            tensor.transpose(PS[:, j * 128:(j + 1) * 128], xn_tok[:, j * 128:(j + 1) * 128], ident[:])
                        for j in range(NEC):
                            m = tensor.transpose(PS[:, 1024 + j * 128:1024 + (j + 1) * 128],
                                                 xq_tok[:, j * 128:(j + 1) * 128], ident[:])
                        c_te.inc(m)

                    @block.vector
                    def _(vector, a=a):
                        c_te.wait(vector)
                        vector.wait_ge(c_wr.sem, c_wr.n)
                        vector.tensor_copy(
                            out=xn_fm[:, :].rearrange("p (j t) -> p j t", t=T)[:, :, a:a + 128],
                            in_=PS[:, 0:1024].rearrange("p (j t) -> p j t", t=128))
                        v = vector.tensor_copy(
                            out=xqfm_sb[:, :],
                            in_=PS[:, 1024:2048].rearrange("p (j t) -> p j t", t=128))
                        c_ve.inc(v)

                    @block.gpsimd
                    def _(gpsimd, a=a):
                        c_ve.wait(gpsimd)
                        g = gpsimd.dma_start(out=xq_dram.ap()[:, :, a:a + 128],
                                             in_=xqfm_sb[:, :].rearrange("p (j t) -> p j t", t=128))
                        c_wr.inc(g, 16)

                barrier()

            with ExitStack() as ph:
                wsb = ph.enter_context(nc.sbuf_tensor(f"wsb{s}", [128, NEC * E], F32))
                stg = ph.enter_context(nc.sbuf_tensor(f"stg{s}", [128, 1024], F32))
                stg2 = ph.enter_context(nc.sbuf_tensor(f"stg2{s}", [128, 2048], F32))

                for wi, (w_in, w_dr) in enumerate([(wq_in, q_dram), (wk_in, k_dram)]):
                    @block.sync
                    def _(sync, w_in=w_in):
                        for ec in range(NEC):
                            c_ld.inc(sync.dma_start(out=wsb[:, ec * E:(ec + 1) * E],
                                                    in_=w_in.ap()[ec * 128:(ec + 1) * 128, :]), 16)

                    for dc in range(NEC):
                        @block.tensor
                        def _(tensor, dc=dc):
                            c_ld.wait(tensor)
                            if dc > 0:
                                tensor.wait_ge(c_ve.sem, c_ve.n)
                            for ec in range(NEC):
                                for fs in range(4):
                                    m = tensor.matmul(PS[:, fs * 512:(fs + 1) * 512],
                                                      wsb[:, ec * E + dc * 128: ec * E + (dc + 1) * 128],
                                                      xn_fm[:, ec * T + fs * 512: ec * T + (fs + 1) * 512],
                                                      start=(ec == 0), stop=(ec == NEC - 1))
                            c_te.inc(m)

                        @block.vector
                        def _(vector, dc=dc):
                            c_te.wait(vector)
                            vector.wait_ge(c_wr.sem, c_wr.n)
                            v = vector.tensor_copy(out=stg2[:, :], in_=PS[:, 0:2048])
                            c_ve.inc(v)

                        @block.gpsimd
                        def _(gpsimd, w_dr=w_dr, dc=dc):
                            c_ve.wait(gpsimd)
                            c_wr.inc(gpsimd.dma_start(out=w_dr.ap()[:, dc, :], in_=stg2[:, :]), 16)
                    barrier()

                @block.sync
                def _(sync):
                    for ec in range(NEC):
                        c_ld.inc(sync.dma_start(out=wsb[:, ec * E:(ec + 1) * E],
                                                in_=wv_in.ap()[ec * 128:(ec + 1) * 128, :]), 16)

                for i in range(NT):
                    a = i * 128

                    @block.tensor
                    def _(tensor, a=a, i=i):
                        c_ld.wait(tensor)
                        if i > 0:
                            tensor.wait_ge(c_ve.sem, c_ve.n)
                        po = (i % 2) * 1024
                        for ec in range(NEC):
                            for vf in range(2):
                                m = tensor.matmul(PS[:, po + vf * 512:po + (vf + 1) * 512],
                                                  xn_fm[:, ec * T + a: ec * T + a + 128],
                                                  wsb[:, ec * E + vf * 512: ec * E + (vf + 1) * 512],
                                                  start=(ec == 0), stop=(ec == NEC - 1))
                        c_te.inc(m)

                    @block.vector
                    def _(vector, i=i):
                        c_te.wait(vector)
                        vector.wait_ge(c_wr.sem, c_wr.n)
                        po = (i % 2) * 1024
                        v = vector.tensor_copy(out=stg[:, 0:1024], in_=PS[:, po:po + 1024])
                        c_ve.inc(v)

                    @block.gpsimd
                    def _(gpsimd, a=a):
                        c_ve.wait(gpsimd)
                        c_wr.inc(gpsimd.dma_start(out=v_dram.ap()[a:a + 128, :], in_=stg[:, :]), 16)
                barrier()

            sub_ab.close()

            scope_cf = ExitStack()
            if True:
                ph = scope_cf
                qring = [ph.enter_context(nc.sbuf_tensor(f"qr{s}{i}", [128, NEC * 128], F32)) for i in range(2)]
                rstage = ph.enter_context(nc.sbuf_tensor(f"rst{s}", [128, 8 * 512], F32))
                qrt16 = ph.enter_context(nc.sbuf_tensor(f"qrt16{s}", [128, H * 128], F32))

                @block.vector
                def _(vector):
                    for hh in range(7):
                        vector.memset(rstage[:, hh * 512 + 257: (hh + 1) * 512], NEG)
                    vector.memset(rstage[:, 7 * 512 + 257: 8 * 512], NEG).then_inc(c_ve.sem, 1)
                c_ve.n += 1

                for i in range(NT):
                    a = i * 128
                    qb = qring[i % 2]

                    @block.sync
                    def _(sync, qb=qb, a=a, i=i):
                        if i >= 2:
                            sync.wait_ge(c_te.sem, c_te.n)
                        sync.wait_ge(c_wr.sem, c_wr.n)
                        r = sync.dma_start(out=qb[:, :].rearrange("p (j t) -> p j t", t=128),
                                           in_=q_dram.ap()[:, :, a:a + 128])
                        c_qr[i % 2].inc(r, 16)

                    @block.vector
                    def _(vector, qb=qb, i=i):
                        c_qr[i % 2].wait(vector)
                        c_rd.wait(vector)
                        vector.wait_ge(c_te.sem, c_te.n)
                        last = None
                        for h in range(H):
                            off = (h % 2) * 64
                            jj = h // 2
                            last = vector.tensor_scalar(out=qrt16[off:off + 64, h * 128:(h + 1) * 128],
                                                        in0=qb[off:off + 64, jj * 128:(jj + 1) * 128],
                                                        scalar1=rb_sb[off:off + 64, h:h + 1],
                                                        scalar2=SC, op0=ALU.add, op1=ALU.mult)
                        c_ve.inc(last)

                    for g in range(2):
                        for q4 in range(4):
                            @block.tensor
                            def _(tensor, g=g, q4=q4, i=i):
                                c_ve.wait(tensor)
                                for k in range(2):
                                    hh = q4 * 2 + k
                                    h = g * 8 + hh
                                    off = (h % 2) * 64
                                    m = tensor.matmul(PS[:, 2048 + k * 512: 2048 + k * 512 + 257],
                                                      qrt16[off:off + 64, h * 128:(h + 1) * 128],
                                                      pt_sb[off:off + 64, :],
                                                      start=True, stop=True)
                                c_te.inc(m)

                            @block.vector
                            def _(vector, g=g, q4=q4, i=i):
                                c_te.wait(vector)
                                if q4 == 0:
                                    vector.wait_ge(c_rho.sem, c_rho.n)
                                if i == 0 or i == NT - 1:
                                    edge = edge0_sb if i == 0 else edge15_sb
                                    last = None
                                    for k in range(2):
                                        hh = q4 * 2 + k
                                        last = vector.tensor_add(out=rstage[:, hh * 512:hh * 512 + 257],
                                                                 in0=PS[:, 2048 + k * 512:2048 + k * 512 + 257],
                                                                 in1=edge[:, :])
                                    c_ve.inc(last)
                                else:
                                    v = vector.tensor_copy(
                                        out=rstage[:, :].rearrange("p (h c) -> p h c", c=512)[:, q4 * 2:(q4 + 1) * 2, 0:257],
                                        in_=PS[:, 2048:].rearrange("p (b c) -> p b c", c=512)[:, 0:2, 0:257])
                                    c_ve.inc(v)

                        @block.gpsimd
                        def _(gpsimd, g=g, a=a):
                            c_ve.wait(gpsimd)
                            w = gpsimd.dma_start(
                                out=bass.AP(rho_dram, (g * 8) * T * 512 + a * 512,
                                            [[512, 128], [T * 512, 8], [1, 512]]),
                                in_=rstage[:, :].rearrange("p (h c) -> p h c", c=512))
                            c_rho.inc(w, 16)

            with ExitStack() as ph:
                w1q_sb = ph.enter_context(nc.sbuf_tensor(f"w1q{s}", [128, NEC * DH], BF16))
                xqr = [ph.enter_context(nc.sbuf_tensor(f"xqr{s}{i}", [128, NEC * 258], BF16)) for i in range(2)]
                hcb = [ph.enter_context(nc.sbuf_tensor(f"hcb{s}{i}", [128, 258], F32)) for i in range(2)]
                ctm = [ph.enter_context(nc.sbuf_tensor(f"ctm{s}{i}", [128, 256], F32)) for i in range(2)]
                hss = ph.enter_context(nc.sbuf_tensor(f"hss{s}", [128, NCC * 256], F32))
                qy = ph.enter_context(nc.sbuf_tensor(f"qy{s}", [128, 1024], F32))
                qc = ph.enter_context(nc.sbuf_tensor(f"qc{s}", [128, 1024], I32))
                qd = ph.enter_context(nc.sbuf_tensor(f"qd{s}", [128, 1024], F32))
                qe = ph.enter_context(nc.sbuf_tensor(f"qe{s}", [128, 1024], F32))
                x2t = ph.enter_context(nc.sbuf_tensor(f"x2t{s}", [128, 1024], F32))
                x2fm = ph.enter_context(nc.sbuf_tensor(f"x2fm{s}", [128, NCC * 128], BF16))
                abc = ph.enter_context(nc.sbuf_tensor(f"abc{s}", [128, 258], F32))
                mx4 = ph.enter_context(nc.sbuf_tensor(f"mx4{s}", [128, 8], F32))
                cxs = ph.enter_context(nc.sbuf_tensor(f"cxs{s}", [128, NT], F32))
                cxT = ph.enter_context(nc.sbuf_tensor(f"cxT{s}", [NT, 128], F32))

                @block.sync
                def _(sync):
                    sync.wait_ge(c_wr.sem, c_wr.n)
                    for ec in range(NEC):
                        c_ld.inc(sync.dma_start(out=w1q_sb[:, :].rearrange("p (j h) -> p j h", h=DH)[:, ec, :],
                                                in_=w1q_dram.ap()[:, ec, :]), 16)

                @block.vector
                def _(vector):
                    v = vector.tensor_scalar(out=cxs[:, :], in0=cxbuf[:, :], scalar1=sca[:, 1:2],
                                             scalar2=None, op0=ALU.mult)
                    c_ve.inc(v)

                @block.tensor
                def _(tensor):
                    c_ve.wait(tensor)
                    c_ld.wait(tensor)
                    m = tensor.transpose(PS[0:NT, 0:128], cxs[:, 0:NT], ident[:])
                    c_te.inc(m)

                @block.vector
                def _(vector):
                    c_te.wait(vector)
                    vector.tensor_copy(out=cxT[:, :], in_=PS[0:NT, 0:128])
                    vector.memset(arow[0:1, 0:1], 0.0)
                    v = vector.memset(arow[0:1, T + 1:T + 2], 0.0)
                    c_ve.inc(v)

                @block.gpsimd
                def _(gpsimd):
                    c_ve.wait(gpsimd)
                    g = gpsimd.dma_start(out=arow[0:1, 1:T + 1].rearrange("p (i t) -> p i t", t=128),
                                         in_=cxT[:, :])
                    c_rd.inc(g, 16)

                NSU = T // 256
                for u in range(NSU):
                    a = u * 256
                    xb = xqr[u % 2]

                    @block.sync
                    def _(sync, xb=xb, a=a, u=u):
                        if u >= 2:
                            sync.wait_ge(c_te.sem, c_te.n)
                        lo = max(a - 1, 0)
                        hi = min(a + 257, T)
                        dst_off = lo - (a - 1)
                        r = sync.dma_start(
                            out=xb[:, :].rearrange("p (j t) -> p j t", t=258)[:, :, dst_off:dst_off + hi - lo],
                            in_=xq_dram.ap()[:, :, lo:hi])
                        c_f[u % 2].inc(r, 16)

                    @block.vector
                    def _(vector, xb=xb, u=u):
                        if u == 0:
                            vector.memset(xb[:, :].rearrange("p (j t) -> p j t", t=258)[:, :, 0:1], 0.0)
                        if u == NSU - 1:
                            vector.memset(xb[:, :].rearrange("p (j t) -> p j t", t=258)[:, :, 257:258], 0.0)
                        vector.memset(barv[0:1, 4:5], 0.0).then_inc(c_ve.sem, 1)
                    c_ve.n += 1

                    @block.tensor
                    def _(tensor, a=a, u=u):
                        c_ve.wait(tensor)
                        c_rd.wait(tensor)
                        m = tensor.matmul(PS[:, 3072:3072 + 258], ones_sb[0:1, :],
                                          arow[0:1, a:a + 258], start=True, stop=True)
                        c_te.inc(m)

                    @block.vector
                    def _(vector):
                        c_te.wait(vector)
                        v = vector.tensor_copy(out=abc[:, :], in_=PS[:, 3072:3072 + 258])
                        c_ve.inc(v)

                    veS = c_ve.n
                    gpS = c_gp.n
                    teS = c_te.n

                    @block.tensor
                    def _(tensor, xb=xb, u=u, veS=veS):
                        c_f[u % 2].wait(tensor)
                        for cc in range(NCC):
                            tensor.wait_ge(c_ve.sem, veS + (4 * (cc - 2) + 1 if cc >= 2 else 0))
                            po = 1024 + (cc % 2) * 512
                            for ec in range(NEC):
                                m = tensor.matmul(PS[:, po:po + 258],
                                                  w1q_sb[:, ec * DH + cc * 128: ec * DH + (cc + 1) * 128],
                                                  xb[:, ec * 258:(ec + 1) * 258],
                                                  start=(ec == 0), stop=(ec == NEC - 1))
                            m.then_inc(c_te.sem, 1)
                    c_te.n += NCC

                    @block.vector
                    def _(vector, teS=teS, seS=None):
                        vector.wait_ge(c_se.sem, c_se.n)
                        for cc in range(NCC):
                            vector.wait_ge(c_te.sem, teS + cc + 1)
                            po = 1024 + (cc % 2) * 512
                            hb = hcb[cc % 2]
                            tb = ctm[cc % 2]
                            w3 = cc * 3
                            vs(vector, vector.tensor_mul(out=hb[:, :], in0=PS[:, po:po + 258], in1=abc[:, :]))
                            vs(vector, vector.tensor_scalar(out=tb[:, :], in0=hb[:, 0:256],
                                                            scalar1=convw_sb[:, w3:w3 + 1],
                                                            scalar2=None, op0=ALU.mult))
                            vs(vector, vector.scalar_tensor_tensor(out=tb[:, :], in0=hb[:, 1:257],
                                                                   scalar=convw_sb[:, w3 + 1:w3 + 2],
                                                                   in1=tb[:, :], op0=ALU.mult, op1=ALU.add))
                            c_ve.inc(vector.scalar_tensor_tensor(out=hss[:, cc * 256:(cc + 1) * 256], in0=hb[:, 2:258],
                                                                 scalar=convw_sb[:, w3 + 2:w3 + 3],
                                                                 in1=tb[:, :], op0=ALU.mult, op1=ALU.add))

                    @block.scalar
                    def _(scalar):
                        c_ve.wait(scalar)
                        last = None
                        for cc in range(NCC):
                            last = scalar.activation(out=hss[:, cc * 256:(cc + 1) * 256],
                                                     in_=hss[:, cc * 256:(cc + 1) * 256], func=AF.Silu,
                                                     bias=convb_sb[:, cc:cc + 1])
                        last.then_inc(c_se.sem, 1)
                    c_se.n += 1

                    for b in range(2):
                        i = u * 2 + b

                        @block.tensor
                        def _(tensor, b=b):
                            c_se.wait(tensor)
                            tensor.wait_ge(c_ve.sem, c_ve.n)
                            for qtr in range(4):
                                for j in range(8):
                                    cc = qtr * 8 + j
                                    m = tensor.transpose(PS[:, qtr * 1024 + j * 128:qtr * 1024 + (j + 1) * 128],
                                                         hss[:, cc * 256 + b * 128: cc * 256 + b * 128 + 128],
                                                         ident[:])
                            c_te.inc(m)

                        @block.vector
                        def _(vector, b=b, i=i):
                            c_te.wait(vector)
                            for qtr in range(4):
                                vs(vector, vector.reduce_max(out=mx4[:, qtr:qtr + 1], in_=PS[:, qtr * 1024:(qtr + 1) * 1024],
                                                             axis=AX.X, apply_absolute_value=True))
                            vs(vector, vector.reduce_max(out=mx4[:, 4:5], in_=mx4[:, 0:4], axis=AX.X))
                            vs(vector, vector.tensor_scalar(out=mx4[:, 5:6], in0=mx4[:, 4:5], scalar1=1e-5,
                                                            scalar2=None, op0=ALU.max))
                            vs(vector, vector.reciprocal(out=mx4[:, 6:7], in_=mx4[:, 5:6]))
                            vs(vector, vector.tensor_scalar(out=mx4[:, 6:7], in0=mx4[:, 6:7], scalar1=127.0,
                                                            scalar2=None, op0=ALU.mult))
                            v = vector.tensor_mul(out=bbuf[:, i:i + 1], in0=mx4[:, 5:6], in1=sca[:, 3:4])
                            c_ve.inc(v)

                        for qtr in range(4):
                            @block.vector
                            def _(vector, qtr=qtr):
                                vector.wait_ge(c_ve.sem, c_ve.n)
                                vs(vector, vector.tensor_scalar(out=qy[:], in0=PS[:, qtr * 1024:(qtr + 1) * 1024],
                                                                scalar1=mx4[:, 6:7],
                                                                scalar2=None, op0=ALU.mult))
                                vs(vector, vector.tensor_copy(out=qc[:], in_=qy[:]))
                                vector.scalar_tensor_tensor(out=qd[:], in0=qc[:], scalar=0.5, in1=qy[:],
                                                            op0=ALU.add, op1=ALU.is_le)
                                vs(vector, vector.scalar_tensor_tensor(out=qe[:], in0=qc[:], scalar=-0.5, in1=qy[:],
                                                                       op0=ALU.add, op1=ALU.is_ge))
                                vs(vector, vector.tensor_sub(out=qe[:], in0=qe[:], in1=qd[:]))
                                v = vector.scalar_tensor_tensor(out=x2t[:], in0=qc[:], scalar=-1.0,
                                                                in1=qe[:], op0=ALU.mult, op1=ALU.add)
                                c_ve.inc(v)

                            @block.tensor
                            def _(tensor, qtr=qtr):
                                c_ve.wait(tensor)
                                for j in range(8):
                                    m = tensor.transpose(PS[:, j * 128:(j + 1) * 128],
                                                         x2t[:, j * 128:(j + 1) * 128], ident[:])
                                c_te.inc(m)

                            @block.vector
                            def _(vector, qtr=qtr):
                                c_te.wait(vector)
                                if qtr == 0:
                                    vector.wait_ge(c_wr.sem, c_wr.n)
                                v = vector.tensor_copy(out=x2fm[:, qtr * 1024:(qtr + 1) * 1024],
                                                       in_=PS[:, 0:1024])
                                c_ve.inc(v)

                        @block.gpsimd
                        def _(gpsimd, a=a, b=b):
                            c_ve.wait(gpsimd)
                            g = gpsimd.dma_start(out=x2q_dram.ap()[:, :, a + b * 128:a + b * 128 + 128],
                                                 in_=x2fm[:, :].rearrange("p (j t) -> p j t", t=128))
                            c_wr.inc(g, 16)
                barrier()

            scope_cf.close()

            with ExitStack() as ph:
                w2q_sb = ph.enter_context(nc.sbuf_tensor(f"w2q{s}", [128, NCC * E], BF16))
                x2r = [ph.enter_context(nc.sbuf_tensor(f"x2r{s}{i}", [128, NCC * 128], BF16)) for i in range(2)]
                ystg = ph.enter_context(nc.sbuf_tensor(f"ystg{s}", [128, E], F32))

                @block.sync
                def _(sync):
                    sync.wait_ge(c_wr.sem, c_wr.n)
                    for cc2 in range(4):
                        c_ld.inc(sync.dma_start(
                            out=w2q_sb[:, :].rearrange("p (j h) -> p j h", h=E)[:, cc2 * 8:(cc2 + 1) * 8, :],
                            in_=w2q_dram.ap()[:, cc2 * 8:(cc2 + 1) * 8, :]), 16)

                for i in range(NT):
                    a = i * 128
                    xb = x2r[i % 2]

                    @block.sync
                    def _(sync, xb=xb, a=a, i=i):
                        if i >= 2:
                            sync.wait_ge(c_te.sem, c_te.n)
                        r = sync.dma_start(out=xb[:, :].rearrange("p (j t) -> p j t", t=128),
                                           in_=x2q_dram.ap()[:, :, a:a + 128])
                        c_g[i % 2].inc(r, 16)

                    @block.tensor
                    def _(tensor, xb=xb, i=i):
                        c_ld.wait(tensor)
                        c_g[i % 2].wait(tensor)
                        if i >= 2:
                            tensor.wait_ge(c_ve.sem, c_ve.n)
                        po = (i % 2) * 1024
                        for cc in range(NCC):
                            tensor.matmul(PS[:, po + 0:po + 512],
                                          xb[:, cc * 128:(cc + 1) * 128],
                                          w2q_sb[:, cc * E:cc * E + 512],
                                          start=(cc == 0), stop=(cc == NCC - 1))
                            m = tensor.matmul(PS[:, po + 512:po + 1024],
                                              xb[:, cc * 128:(cc + 1) * 128],
                                              w2q_sb[:, cc * E + 512:(cc + 1) * E],
                                              start=(cc == 0), stop=(cc == NCC - 1))
                        c_te.inc(m)

                    @block.vector
                    def _(vector, i=i):
                        c_te.wait(vector)
                        vector.wait_ge(c_wr.sem, c_wr.n)
                        po = (i % 2) * 1024
                        v = vector.tensor_scalar(out=ystg[:, :], in0=PS[:, po:po + 1024],
                                                 scalar1=bbuf[:, i:i + 1], scalar2=None, op0=ALU.mult)
                        c_ve.inc(v)

                    @block.gpsimd
                    def _(gpsimd, a=a):
                        c_ve.wait(gpsimd)
                        c_wr.inc(gpsimd.dma_start(out=y2_dram.ap()[a:a + 128, :], in_=ystg[:]), 16)
                barrier()

            with ExitStack() as ph:
                wo_sb = ph.enter_context(nc.sbuf_tensor(f"wo{s}", [128, NEC * E], F32))
                qrg = [ph.enter_context(nc.sbuf_tensor(f"qg{s}{i}", [128, NEC * 128], F32)) for i in range(2)]
                krg = [ph.enter_context(nc.sbuf_tensor(f"kg{s}{i}", [128, NEC * 128], F32)) for i in range(4)]
                vrg = [ph.enter_context(nc.sbuf_tensor(f"vg{s}{i}", [128, E], F32)) for i in range(4)]
                skw2 = [ph.enter_context(nc.sbuf_tensor(f"skw{s}{i}", [128, 8 * 384], F32)) for i in range(2)]
                ssb = ph.enter_context(nc.sbuf_tensor(f"ssb{s}", [128, 8 * 384], F32))
                psb = ph.enter_context(nc.sbuf_tensor(f"psb{s}", [128, 8 * 384], F32))
                ptb = ph.enter_context(nc.sbuf_tensor(f"ptb{s}", [128, 3072], F32))
                qt_sb = ph.enter_context(nc.sbuf_tensor(f"qt{s}", [128, 8 * 128], F32))
                mneg = ph.enter_context(nc.sbuf_tensor(f"mneg{s}", [128, 8], F32))
                zcol = ph.enter_context(nc.sbuf_tensor(f"zcol{s}", [128, 16], F32))
                attok = ph.enter_context(nc.sbuf_tensor(f"attok{s}", [128, E], F32))
                atfm = ph.enter_context(nc.sbuf_tensor(f"atfm{s}", [128, NEC * 128], F32))
                y2r = ph.enter_context(nc.sbuf_tensor(f"y2r{s}", [128, E], F32))
                xmr = ph.enter_context(nc.sbuf_tensor(f"xmr{s}", [128, E], F32))
                otok = ph.enter_context(nc.sbuf_tensor(f"otok{s}", [128, E], F32))

                @block.sync
                def _(sync):
                    sync.wait_ge(c_wr.sem, c_wr.n)
                    for ec in range(NEC):
                        c_ld.inc(sync.dma_start(out=wo_sb[:, ec * E:(ec + 1) * E],
                                                in_=wo_in.ap()[ec * 128:(ec + 1) * 128, :]), 16)

                @block.vector
                def _(vector):
                    vector.memset(krg[3][:, :], 0.0)
                    vector.memset(vrg[3][:, :], 0.0).then_inc(c_ve.sem, 1)
                c_ve.n += 1

                skw_con = []

                def kv_slot(ti):
                    return 3 if (ti < 0 or ti >= NT) else ti % 3

                @block.sync
                def _(sync):
                    for ti in (0, 1):
                        c_kv.inc(sync.dma_start(out=krg[ti % 3][:, :].rearrange("p (j t) -> p j t", t=128),
                                                in_=k_dram.ap()[:, :, ti * 128:(ti + 1) * 128]), 16)
                        c_kv.inc(sync.dma_start(out=vrg[ti % 3][:, :], in_=v_dram.ap()[ti * 128:(ti + 1) * 128, :]), 16)

                for i in range(NT):
                    a = i * 128
                    qb = qrg[i % 2]

                    @block.sync
                    def _(sync, qb=qb, a=a, i=i):
                        if i >= 2:
                            sync.wait_ge(c_te.sem, c_te.n)
                        if i >= 1:
                            sync.wait_ge(c_ve.sem, c_ve.n)
                        c_qr[i % 2].inc(sync.dma_start(out=qb[:, :].rearrange("p (j t) -> p j t", t=128),
                                                       in_=q_dram.ap()[:, :, a:a + 128]), 16)
                        if i + 1 < NT and i > 0:
                            ti = i + 1
                            c_kv.inc(sync.dma_start(out=krg[ti % 3][:, :].rearrange("p (j t) -> p j t", t=128),
                                                    in_=k_dram.ap()[:, :, ti * 128:(ti + 1) * 128]), 16)
                            c_kv.inc(sync.dma_start(out=vrg[ti % 3][:, :],
                                                    in_=v_dram.ap()[ti * 128:(ti + 1) * 128, :]), 16)
                        c_mrg.inc(sync.dma_start(out=xmr[:], in_=x_in.ap()[s, a:a + 128, :]), 16)
                        c_mrg.inc(sync.dma_start(out=y2r[:], in_=y2_dram.ap()[a:a + 128, :]), 16)

                    for g in range(2):
                        gi = i * 2 + g
                        skw = skw2[gi % 2]

                        @block.gpsimd
                        def _(gpsimd, g=g, a=a, i=i, gi=gi, vtgt=None):
                            if gi >= 2:
                                gpsimd.wait_ge(c_te.sem, skw_con[gi - 2])
                            r = gpsimd.dma_start(
                                out=skw[:, :].rearrange("p (h u) -> p h u", u=384),
                                in_=bass.AP(rho_dram, (g * 8) * T * 512 + a * 512,
                                            [[511, 128], [T * 512, 8], [1, 384]]))
                            c_skw2[gi % 2].inc(r, 16)

                        for hh in range(8):
                            h = g * 8 + hh
                            off = (h % 2) * 64
                            jj = h // 2

                            @block.vector
                            def _(vector, qb=qb, h=h, off=off, jj=jj, hh=hh, g=g, i=i):
                                c_qr[i % 2].wait(vector)
                                c_kv.wait(vector)
                                if hh == 0:
                                    vector.wait_ge(c_te.sem, c_te.n)
                                v = vector.tensor_scalar(out=qt_sb[off:off + 64, hh * 128:(hh + 1) * 128],
                                                         in0=qb[off:off + 64, jj * 128:(jj + 1) * 128],
                                                         scalar1=cb_sb[off:off + 64, h:h + 1],
                                                         scalar2=SC, op0=ALU.add, op1=ALU.mult)
                                c_ve.inc(v)

                        @block.tensor
                        def _(tensor, g=g, i=i, gi=gi, skw=skw):
                            c_ve.wait(tensor)
                            tensor.wait_ge(c_se.sem, c_se.n)
                            c_skw2[gi % 2].wait(tensor)
                            for c in range(6):
                                tensor.matmul(PS[:, c * 512:(c + 1) * 512], ident[:],
                                              skw[:, c * 512:(c + 1) * 512], start=True, stop=False)
                            for hh in (0, 2, 4, 6, 1, 3, 5, 7):
                                h = g * 8 + hh
                                off = (h % 2) * 64
                                jj = h // 2
                                for ci, ti in enumerate((i - 1, i, i + 1)):
                                    kb = krg[kv_slot(ti)]
                                    m = tensor.matmul(PS[:, hh * 384 + ci * 128: hh * 384 + (ci + 1) * 128],
                                                      qt_sb[off:off + 64, hh * 128:(hh + 1) * 128],
                                                      kb[off:off + 64, jj * 128:(jj + 1) * 128],
                                                      start=False, stop=True)
                            c_te.inc(m)
                        skw_con.append(c_te.n)

                        @block.vector
                        def _(vector):
                            c_te.wait(vector)
                            for hh in range(8):
                                v = vector.reduce_max(out=mneg[:, hh:hh + 1],
                                                      in_=PS[:, hh * 384:(hh + 1) * 384],
                                                      axis=AX.X, negate=True)
                            c_ve.inc(v)

                        @block.scalar
                        def _(scalar, g=g):
                            c_ve.wait(scalar)
                            scalar.wait_ge(c_te.sem, c_te.n)
                            last = None
                            for hh in range(8):
                                h = g * 8 + hh
                                last = scalar.activation(out=psb[:, hh * 384:(hh + 1) * 384],
                                                         in_=PS[:, hh * 384:(hh + 1) * 384],
                                                         func=AF.Exp, bias=mneg[:, hh:hh + 1],
                                                         accum_out=zcol[:, h:h + 1])
                            last.then_inc(c_se.sem, 1)
                        c_se.n += 1

                        @block.tensor
                        def _(tensor):
                            c_se.wait(tensor)
                            tensor.wait_ge(c_ve.sem, c_ve.n)
                            for k in range(24):
                                m = tensor.transpose(PS[:, k * 128:(k + 1) * 128],
                                                     psb[:, k * 128:(k + 1) * 128], ident[:])
                            c_te.inc(m)

                        @block.vector
                        def _(vector):
                            c_te.wait(vector)
                            v = vector.tensor_copy(out=ptb[:, :], in_=PS[:, 0:3072])
                            c_ve.inc(v)

                        @block.tensor
                        def _(tensor, g=g, i=i):
                            c_ve.wait(tensor)
                            for hh in range(8):
                                h = g * 8 + hh
                                for ci, ti in enumerate((i - 1, i, i + 1)):
                                    vb = vrg[kv_slot(ti)]
                                    m = tensor.matmul(PS[:, 3072 + hh * 64: 3072 + (hh + 1) * 64],
                                                      ptb[:, hh * 384 + ci * 128:hh * 384 + (ci + 1) * 128],
                                                      vb[:, h * 64:(h + 1) * 64],
                                                      start=(ci == 0), stop=(ci == 2))
                            c_te.inc(m)

                        @block.vector
                        def _(vector, g=g):
                            c_te.wait(vector)
                            last = None
                            for hh in range(8):
                                h = g * 8 + hh
                                last = vector.reciprocal(out=zcol[:, h:h + 1], in_=zcol[:, h:h + 1])
                            vs(vector, last)
                            last = None
                            for hh in range(8):
                                h = g * 8 + hh
                                last = vector.tensor_scalar(out=attok[:, h * 64:(h + 1) * 64],
                                                            in0=PS[:, 3072 + hh * 64:3072 + (hh + 1) * 64],
                                                            scalar1=zcol[:, h:h + 1], scalar2=None, op0=ALU.mult)
                            last.then_inc(c_ve.sem, 1)
                        c_ve.n += 1

                    @block.tensor
                    def _(tensor):
                        c_ve.wait(tensor)
                        for j in range(NEC):
                            m = tensor.transpose(PS[:, j * 128:(j + 1) * 128],
                                                 attok[:, j * 128:(j + 1) * 128], ident[:])
                        c_te.inc(m)

                    @block.vector
                    def _(vector):
                        c_te.wait(vector)
                        v = vector.tensor_copy(out=atfm[:], in_=PS[:, 0:1024])
                        c_ve.inc(v)

                    @block.tensor
                    def _(tensor, i=i):
                        c_ve.wait(tensor)
                        c_ld.wait(tensor)
                        for ec in range(NEC):
                            for vf in range(2):
                                m = tensor.matmul(PS[:, 1024 + vf * 512:1024 + (vf + 1) * 512],
                                                  atfm[:, ec * 128:(ec + 1) * 128],
                                                  wo_sb[:, ec * E + vf * 512:ec * E + (vf + 1) * 512],
                                                  start=(ec == 0), stop=(ec == NEC - 1))
                        c_te.inc(m)

                    @block.vector
                    def _(vector, i=i):
                        c_te.wait(vector)
                        c_mrg.wait(vector)
                        vector.wait_ge(c_out.sem, c_out.n)
                        vs(vector, vector.tensor_add(out=otok[:, :], in0=PS[:, 1024:2048], in1=bo_b[:, :]))
                        vs(vector, vector.tensor_add(out=otok[:, :], in0=otok[:, :], in1=xmr[:, :]))
                        v = vector.tensor_add(out=otok[:, :], in0=otok[:, :], in1=y2r[:, :])
                        c_ve.inc(v)

                    @block.gpsimd
                    def _(gpsimd, a=a, i=i):
                        c_ve.wait(gpsimd)
                        c_out.inc(gpsimd.dma_start(out=out_dr.ap()[s, a:a + 128, :], in_=otok[:]), 16)
                barrier()

    @block.gpsimd
    def _(gpsimd):
        gpsimd.wait_ge(c_out.sem, c_out.n)

    stk.close()
    return nc


_ENC = None


def _make_consts():
    global _ENC
    import jax
    import jax.numpy as jnp
    if _ENC is None:
        cpu = jax.local_devices(backend="cpu")[0]
        with jax.default_device(cpu):
            pos = jnp.arange(-T + 1, T, dtype=jnp.float32)
            freqs = jnp.arange(0, D, 2, dtype=jnp.float32)
            inv = 1.0 / 10000.0 ** (freqs / D)
            pe = pos[:, None] * inv[None, :]
            _ENC = np.asarray(jnp.concatenate([jnp.sin(pe), jnp.cos(pe)], axis=-1))
    enc = _ENC
    sel = enc[2175 - np.arange(257), :].T.astype(np.float32)
    encsel2 = np.concatenate([sel, sel], axis=0)
    ident = np.eye(128, dtype=np.float32)
    t_idx = np.arange(128)[:, None]
    m_idx = np.arange(257)[None, :]
    edge0 = np.where(m_idx < 128 - t_idx, np.float32(NEG), np.float32(0.0)).astype(np.float32)
    edge15 = np.where(m_idx >= 256 - t_idx, np.float32(NEG), np.float32(0.0)).astype(np.float32)
    return encsel2, ident, edge0, edge15


def kernel(L0_attn_x, L0_attn_W_q, L0_attn_W_k, L0_attn_W_v, L0_attn_content_bias,
           L0_attn_relative_bias, L0_attn_W_rel, L0_attn_b_rel, L0_attn_W_o, L0_attn_b_o,
           L0_ffn_w1, L0_ffn_conv_w, L0_ffn_conv_b, L0_ffn_w2):
    x = np.ascontiguousarray(np.asarray(L0_attn_x, dtype=np.float32))
    encsel2, ident, edge0, edge15 = _make_consts()

    cbT = np.asarray(L0_attn_content_bias, np.float32).T
    rbT = np.asarray(L0_attn_relative_bias, np.float32).T
    cbT2 = np.concatenate([cbT, cbT], axis=0)
    rbT2 = np.concatenate([rbT, rbT], axis=0)
    wrel = np.asarray(L0_attn_W_rel, np.float32)
    wrel2 = np.concatenate([wrel, wrel], axis=0)
    brel = np.asarray(L0_attn_b_rel, np.float32).reshape(64, 1)
    brel2 = np.concatenate([brel, brel], axis=0)
    bo = np.asarray(L0_attn_b_o, np.float32).reshape(1, E)
    w1t = np.ascontiguousarray(np.asarray(L0_ffn_w1, np.float32).T)
    w2t = np.ascontiguousarray(np.asarray(L0_ffn_w2, np.float32).T)
    cw = np.asarray(L0_ffn_conv_w, np.float32).reshape(DH, 3)
    convw = np.ascontiguousarray(cw.reshape(NCC, 128, 3).transpose(1, 0, 2).reshape(128, NCC * 3))
    convb = np.ascontiguousarray(np.asarray(L0_ffn_conv_b, np.float32).reshape(NCC, 128).T)

    common = {
        "wq": np.ascontiguousarray(np.asarray(L0_attn_W_q, np.float32)),
        "wk": np.ascontiguousarray(np.asarray(L0_attn_W_k, np.float32)),
        "wv": np.ascontiguousarray(np.asarray(L0_attn_W_v, np.float32)),
        "wo": np.ascontiguousarray(np.asarray(L0_attn_W_o, np.float32)),
        "cbT2": np.ascontiguousarray(cbT2), "rbT2": np.ascontiguousarray(rbT2),
        "wrel2": np.ascontiguousarray(wrel2), "brel2": np.ascontiguousarray(brel2),
        "bo": np.ascontiguousarray(bo),
        "w1t": w1t, "w2t": w2t, "convw": convw, "convb": convb,
        "encsel2": np.ascontiguousarray(encsel2), "ident": ident,
        "edge0": np.ascontiguousarray(edge0), "edge15": np.ascontiguousarray(edge15),
    }
    nc = build_nc()
    in_maps = []
    for c in range(8):
        m = dict(common)
        m["x"] = np.ascontiguousarray(x[2 * c:2 * c + 2])
        in_maps.append(m)
    res = run_bass_kernel_spmd(nc, in_maps, core_ids=list(range(8)))
    out = np.concatenate([r["out"].reshape(NSEQ, T, E) for r in res.results], axis=0)
    return out.astype(np.float32)


# version 23 (final)
# speedup vs baseline: 1.0715x; 1.0715x over previous
import numpy as np
from contextlib import ExitStack

import concourse.bass as bass
import concourse.mybir as mybir
import concourse.bass_utils as _bass_utils
from concourse.bass_utils import run_bass_kernel_spmd

if not getattr(_bass_utils, "_ldwopt_patched", False):
    _orig_run_command = _bass_utils.run_command

    def _run_command_ldwopt(cmd, *a, **kw):
        if isinstance(cmd, list):
            cmd = ["--enable-ldw-opt=true" if c == "--enable-ldw-opt=false" else c for c in cmd]
        return _orig_run_command(cmd, *a, **kw)

    _bass_utils.run_command = _run_command_ldwopt
    _bass_utils._ldwopt_patched = True

F32 = mybir.dt.float32
BF16 = mybir.dt.bfloat16
I32 = mybir.dt.int32
AF = mybir.ActivationFunctionType
ALU = mybir.AluOpType
AX = mybir.AxisListType

T = 2048
E = 1024
H = 16
D = 64
DH = 4096
NSEQ = 2
NT = T // 128
NEC = E // 128
NCC = DH // 128
SC = 1.0 / 32.0
NEG = -10000.0


class Ctr:
    def __init__(self, stk, nc, name):
        self.sem = stk.enter_context(nc.semaphore(name))
        self.n = 0

    def inc(self, instr, k=1):
        instr.then_inc(self.sem, k)
        self.n += k
        return instr

    def wait(self, eng, n=None):
        eng.wait_ge(self.sem, self.n if n is None else n)


def build_nc():
    nc = bass.Bass("TRN2", target_bir_lowering=False)

    dr = lambda name, shape, dt=F32, kind="Internal": nc.dram_tensor(name, shape, dt, kind=kind)

    x_in = dr("x", [NSEQ, T, E], kind="ExternalInput")
    wq_in = dr("wq", [E, E], kind="ExternalInput")
    wk_in = dr("wk", [E, E], kind="ExternalInput")
    wv_in = dr("wv", [E, E], kind="ExternalInput")
    wo_in = dr("wo", [E, E], kind="ExternalInput")
    cb_in = dr("cbT2", [128, H], kind="ExternalInput")
    rb_in = dr("rbT2", [128, H], kind="ExternalInput")
    wrel_in = dr("wrel2", [128, D], kind="ExternalInput")
    brel_in = dr("brel2", [128, 1], kind="ExternalInput")
    bo_in = dr("bo", [1, E], kind="ExternalInput")
    w1t_in = dr("w1t", [E, DH], kind="ExternalInput")
    w2t_in = dr("w2t", [DH, E], kind="ExternalInput")
    convw_in = dr("convw", [128, NCC * 3], kind="ExternalInput")
    convb_in = dr("convb", [128, NCC], kind="ExternalInput")
    enc_in = dr("encsel2", [128, 257], kind="ExternalInput")
    ident_in = dr("ident", [128, 128], kind="ExternalInput")
    edge0_in = dr("edge0", [128, 257], kind="ExternalInput")
    edge15_in = dr("edge15", [128, 257], kind="ExternalInput")
    out_dr = dr("out", [NSEQ, T, E], kind="ExternalOutput")

    q_dram = dr("q_s", [128, NEC, T])
    k_dram = dr("k_s", [128, NEC, T])
    v_dram = dr("v_s", [T, E])
    xq_dram = dr("xq_s", [128, NEC, T], BF16)
    x2q_dram = dr("x2q_s", [128, NCC, T], BF16)
    y2_dram = dr("y2_s", [T, E])
    rho_dram = dr("rho_s", [H, T, 512])
    w1q_dram = dr("w1q_s", [128, NEC, DH], BF16)
    w2q_dram = dr("w2q_s", [128, NCC, E], BF16)

    stk = ExitStack()
    sb = lambda name, shape, dt=F32: stk.enter_context(nc.sbuf_tensor(name, shape, dt))

    ident = sb("identt", [128, 128])
    cb_sb = sb("cb", [128, H])
    rb_sb = sb("rb", [128, H])
    wrel_sb = sb("wrelb", [128, D])
    brel_sb = sb("brelb", [128, 1])
    enc_sb = sb("encb", [128, 257])
    edge0_sb = sb("edge0b", [128, 257])
    edge15_sb = sb("edge15b", [128, 257])
    convb_sb = sb("convbb", [128, NCC])
    convw_sb = sb("convwb", [128, NCC * 3])
    pt_sb = sb("ptband", [128, 257])
    bo_b = sb("bo_b", [128, E])
    ones_sb = sb("ones", [1, 128])
    sca = sb("sca", [128, 8])
    bo_row = sb("bo_row", [1, E])
    zero_sb = sb("zeroc", [128, 1])
    eps_sb = sb("epsc", [128, 1])
    barv = sb("barv", [1, 8])

    PS = stk.enter_context(nc.psum_tensor("PS", [128, 4096], F32))

    c_ld = Ctr(stk, nc, "c_ld")
    c_wr = Ctr(stk, nc, "c_wr")
    c_rd = Ctr(stk, nc, "c_rd")
    c_te = Ctr(stk, nc, "c_te")
    c_ve = Ctr(stk, nc, "c_ve")
    c_se = Ctr(stk, nc, "c_se")
    c_out = Ctr(stk, nc, "c_out")
    c_bar = Ctr(stk, nc, "c_bar")
    c_x = [Ctr(stk, nc, "c_x0"), Ctr(stk, nc, "c_x1")]
    c_f = [Ctr(stk, nc, "c_f0"), Ctr(stk, nc, "c_f1")]
    c_g = [Ctr(stk, nc, "c_g0"), Ctr(stk, nc, "c_g1")]
    c_qr = [Ctr(stk, nc, "c_qr0"), Ctr(stk, nc, "c_qr1")]
    c_kv = Ctr(stk, nc, "c_kv")
    c_skw2 = [Ctr(stk, nc, f"c_skw{i}") for i in range(2)]
    c_mrg = Ctr(stk, nc, "c_mrg")
    c_gp = Ctr(stk, nc, "c_gp")
    c_rho = Ctr(stk, nc, "c_rho")

    block = stk.enter_context(nc.Block())

    def vs(vector, instr):
        c_ve.inc(instr)
        vector.wait_ge(c_ve.sem, c_ve.n)

    ENGS = None

    def barrier():
        c_bar.n += 5
        tgt = c_bar.n

        @block.sync
        def _(eng, tgt=tgt):
            for ctr in (c_ld, c_wr, c_rd, c_out, c_kv, c_skw2[0], c_skw2[1], c_mrg, c_rho,
                        c_x[0], c_x[1], c_f[0], c_f[1], c_g[0], c_g[1],
                        c_qr[0], c_qr[1]):
                if ctr.n:
                    eng.wait_ge(ctr.sem, ctr.n)
            eng.nop().then_inc(c_bar.sem, 1)
            eng.wait_ge(c_bar.sem, tgt)

        @block.vector
        def _(eng, tgt=tgt):
            eng.memset(barv[0:1, 0:1], 0.0).then_inc(c_bar.sem, 1)
            eng.wait_ge(c_bar.sem, tgt)

        @block.scalar
        def _(eng, tgt=tgt):
            eng.activation(out=barv[0:1, 1:2], in_=zero_sb[0:1, 0:1], func=AF.Relu,
                           bias=zero_sb[0:1, 0:1]).then_inc(c_bar.sem, 1)
            eng.wait_ge(c_bar.sem, tgt)

        @block.tensor
        def _(eng, tgt=tgt):
            eng.wait_ge(c_ve.sem, c_ve.n)
            eng.wait_ge(c_se.sem, c_se.n)
            eng.matmul(PS[0:1, 4095:4096], ones_sb[0:1, 0:1], ones_sb[0:1, 0:1],
                       start=True, stop=True).then_inc(c_bar.sem, 1)
            eng.wait_ge(c_bar.sem, tgt)

        @block.gpsimd
        def _(eng, tgt=tgt):
            eng.memset(barv[0:1, 3:4], 0.0).then_inc(c_bar.sem, 1)
            eng.wait_ge(c_bar.sem, tgt)

    @block.sync
    def _(sync):
        for dst, src in [(ident, ident_in), (cb_sb, cb_in), (rb_sb, rb_in),
                         (wrel_sb, wrel_in), (brel_sb, brel_in), (enc_sb, enc_in),
                         (edge0_sb, edge0_in), (edge15_sb, edge15_in),
                         (convb_sb, convb_in), (convw_sb, convw_in), (bo_row, bo_in)]:
            c_ld.inc(sync.dma_start(out=dst[:], in_=src.ap()), 16)

    @block.vector
    def _(vector):
        vector.memset(ones_sb[:], 1.0)
        vector.memset(zero_sb[:], 0.0)
        vector.memset(eps_sb[:], 1e-5).then_inc(c_ve.sem, 1)
    c_ve.n += 1

    @block.scalar
    def _(scalar):
        scalar.wait_ge(c_ve.sem, 1)

    @block.tensor
    def _(tensor):
        c_ld.wait(tensor)
        c_ve.wait(tensor)
        tensor.matmul(PS[0:64, 0:257], wrel_sb[0:64, :], enc_sb[0:64, :], start=True, stop=True)
        tensor.matmul(PS[:, 512:1024], ones_sb[0:1, :], bo_row[0:1, 0:512], start=True, stop=True)
        m = tensor.matmul(PS[:, 1024:1536], ones_sb[0:1, :], bo_row[0:1, 512:1024], start=True, stop=True)
        c_te.inc(m)

    @block.vector
    def _(vector):
        c_te.wait(vector)
        vector.tensor_scalar(out=pt_sb[0:64, :], in0=PS[0:64, 0:257], scalar1=brel_sb[0:64, 0:1],
                             scalar2=None, op0=ALU.add)
        vector.tensor_copy(out=bo_b[:, 0:512], in_=PS[:, 512:1024])
        v = vector.tensor_copy(out=bo_b[:, 512:1024], in_=PS[:, 1024:1536])
        c_ve.inc(v)

    @block.gpsimd
    def _(gpsimd):
        c_ve.wait(gpsimd)
        c_rd.inc(gpsimd.dma_start(out=pt_sb[64:128, :], in_=pt_sb[0:64, :]), 16)

    barrier()

    with ExitStack() as p1:
        wbuf = p1.enter_context(nc.sbuf_tensor("wbuf", [128, 4096], F32))
        wqb = p1.enter_context(nc.sbuf_tensor("wqb", [128, 4096], BF16))
        dbuf = p1.enter_context(nc.sbuf_tensor("dbuf", [128, 4096], F32))
        ebuf = p1.enter_context(nc.sbuf_tensor("ebuf", [128, 4096], F32))
        acc = p1.enter_context(nc.sbuf_tensor("acc", [128, 2], F32))
        tiny = p1.enter_context(nc.sbuf_tensor("tiny", [128, 8], F32))

        for wi, (w_in, wq_dr, nch, ncols, scol, mcol, nelem) in enumerate([
                (w1t_in, w1q_dram, NEC, DH, 0, 1, E * DH),
                (w2t_in, w2q_dram, NCC, E, 2, 3, DH * E)]):
            @block.sync
            def _(sync, w_in=w_in, nch=nch, ncols=ncols, vbase=c_ve.n):
                for c in range(nch):
                    c_ld.inc(sync.dma_start(out=wbuf[:, 0:ncols], in_=w_in.ap()[c * 128:(c + 1) * 128, :]), 16)
                    sync.wait_ge(c_ve.sem, vbase + 2 * c + 1)
            v0 = c_ve.n

            @block.vector
            def _(vector, nch=nch, ncols=ncols, base=c_ld.n - 16 * nch):
                for c in range(nch):
                    vector.wait_ge(c_ld.sem, base + 16 * (c + 1))
                    vs(vector, vector.tensor_reduce(out=acc[:, 1:2], in_=wbuf[:, 0:ncols], axis=AX.X,
                                                    op=ALU.add, apply_absolute_value=True))
                    if c == 0:
                        vs(vector, vector.tensor_copy(out=acc[:, 0:1], in_=acc[:, 1:2]))
                    else:
                        vs(vector, vector.tensor_add(out=acc[:, 0:1], in0=acc[:, 0:1], in1=acc[:, 1:2]))

            @block.tensor
            def _(tensor):
                c_ve.wait(tensor)
                c_te.inc(tensor.transpose(PS[0:1, 0:128], acc[:, 0:1], ident[:]))

            @block.vector
            def _(vector, scol=scol, mcol=mcol, nelem=nelem):
                c_te.wait(vector)
                vs(vector, vector.reduce_sum(out=tiny[0:1, 0:1], in_=PS[0:1, 0:128], axis=AX.X))
                vs(vector, vector.tensor_scalar(out=tiny[0:1, 1:2], in0=tiny[0:1, 0:1], scalar1=1.0 / nelem,
                                                scalar2=1e-5, op0=ALU.mult, op1=ALU.max))
                vs(vector, vector.reciprocal(out=tiny[0:1, 2:3], in_=tiny[0:1, 1:2]))
                v = vector.tensor_scalar(out=tiny[0:1, 3:4], in0=tiny[0:1, 1:2], scalar1=1.0 / 127.0,
                                         scalar2=None, op0=ALU.mult)
                c_ve.inc(v)

            @block.tensor
            def _(tensor, scol=scol):
                c_ve.wait(tensor)
                tensor.matmul(PS[:, 256:257], ones_sb[0:1, :], tiny[0:1, 2:3], start=True, stop=True)
                m = tensor.matmul(PS[:, 257:258], ones_sb[0:1, :], tiny[0:1, 3:4], start=True, stop=True)
                c_te.inc(m)

            @block.vector
            def _(vector, scol=scol, mcol=mcol):
                c_te.wait(vector)
                vs(vector, vector.tensor_copy(out=sca[:, scol:scol + 1], in_=PS[:, 256:257]))
                vs(vector, vector.tensor_copy(out=sca[:, mcol:mcol + 1], in_=PS[:, 257:258]))

            @block.sync
            def _(sync, w_in=w_in, nch=nch, ncols=ncols, vbase2=c_ve.n):
                for c in range(nch):
                    sync.wait_ge(c_wr.sem, c_wr.n + 16 * c)
                    c_ld.inc(sync.dma_start(out=wbuf[:, 0:ncols], in_=w_in.ap()[c * 128:(c + 1) * 128, :]), 16)
                    sync.wait_ge(c_ve.sem, vbase2 + 2 * (c + 1))

            @block.vector
            def _(vector, nch=nch, ncols=ncols, scol=scol, base=c_ld.n - 16 * nch, base_wr=c_wr.n):
                for c in range(nch):
                    vector.wait_ge(c_ld.sem, base + 16 * (c + 1))
                    if c > 0:
                        vector.wait_ge(c_wr.sem, base_wr + 16 * c)
                    vector.tensor_scalar(out=dbuf[:, 0:ncols], in0=wbuf[:, 0:ncols],
                                         scalar1=sca[:, scol:scol + 1], scalar2=0.5,
                                         op0=ALU.mult, op1=ALU.is_ge)
                    vs(vector, vector.tensor_scalar(out=ebuf[:, 0:ncols], in0=wbuf[:, 0:ncols],
                                                    scalar1=sca[:, scol:scol + 1], scalar2=-0.5,
                                                    op0=ALU.mult, op1=ALU.is_le))
                    v = vector.tensor_sub(out=wqb[:, 0:ncols], in0=ebuf[:, 0:ncols], in1=dbuf[:, 0:ncols])
                    c_ve.inc(v)

            @block.gpsimd
            def _(gpsimd, wq_dr=wq_dr, nch=nch, ncols=ncols, base=c_ve.n - 2 * nch):
                for c in range(nch):
                    gpsimd.wait_ge(c_ve.sem, base + 2 * (c + 1))
                    c_wr.inc(gpsimd.dma_start(out=wq_dr.ap()[:, c, :], in_=wqb[:, 0:ncols]), 16)
            barrier()

    for s in range(NSEQ):
        with ExitStack() as sq:
            cxbuf = sq.enter_context(nc.sbuf_tensor(f"cx{s}", [128, NT], F32))
            bbuf = sq.enter_context(nc.sbuf_tensor(f"bb{s}", [128, NT], F32))
            arow = sq.enter_context(nc.sbuf_tensor(f"ar{s}", [1, T + 2], F32))

            sub_ab = ExitStack()
            xn_fm = sub_ab.enter_context(nc.sbuf_tensor(f"xnfm{s}", [128, NEC * T], F32))

            with ExitStack() as ph:
                x_tok = [ph.enter_context(nc.sbuf_tensor(f"xt{s}{i}", [128, E], F32)) for i in range(2)]
                xn_tok = ph.enter_context(nc.sbuf_tensor(f"xnt{s}", [128, E], F32))
                xq_tok = ph.enter_context(nc.sbuf_tensor(f"xqt{s}", [128, E], F32))
                ci_t = ph.enter_context(nc.sbuf_tensor(f"ci{s}", [128, E], I32))
                d_t = ph.enter_context(nc.sbuf_tensor(f"d{s}", [128, E], F32))
                e_t = ph.enter_context(nc.sbuf_tensor(f"e{s}", [128, E], F32))
                sqd = ph.enter_context(nc.sbuf_tensor(f"sqd{s}", [128, E], F32))
                st_ = ph.enter_context(nc.sbuf_tensor(f"st{s}", [128, 8], F32))
                xqfm_sb = ph.enter_context(nc.sbuf_tensor(f"xqfm{s}", [128, NEC * 128], BF16))

                for i in range(NT):
                    a = i * 128
                    buf = x_tok[i % 2]

                    @block.sync
                    def _(sync, buf=buf, i=i, a=a):
                        if i >= 2:
                            sync.wait_ge(c_ve.sem, c_ve.n)
                        c_x[i % 2].inc(sync.dma_start(out=buf[:], in_=x_in.ap()[s, a:a + 128, :]), 16)

                    @block.vector
                    def _(vector, buf=buf, i=i):
                        c_x[i % 2].wait(vector)
                        vector.scalar_tensor_tensor(out=sqd[:], in0=buf[:], scalar=1.0, in1=buf[:],
                                                    op0=ALU.mult, op1=ALU.mult, accum_out=st_[:, 0:1])
                        vs(vector, vector.reduce_sum(out=st_[:, 1:2], in_=buf[:], axis=AX.X))
                        vs(vector, vector.scalar_tensor_tensor(out=st_[:, 2:3], in0=st_[:, 1:2], scalar=1.0 / E,
                                                               in1=st_[:, 1:2], op0=ALU.mult, op1=ALU.mult))
                        vector.tensor_sub(out=st_[:, 3:4], in0=st_[:, 0:1], in1=st_[:, 2:3])
                        v = vector.tensor_scalar(out=st_[:, 4:5], in0=st_[:, 1:2], scalar1=-1.0 / E,
                                                 scalar2=None, op0=ALU.mult)
                        c_ve.inc(v)

                    @block.scalar
                    def _(scalar):
                        c_ve.wait(scalar)
                        ss = scalar.activation(out=st_[:, 5:6], in_=st_[:, 3:4], func=AF.Sqrt,
                                               bias=eps_sb[:, 0:1], scale=1.0 / E)
                        c_se.inc(ss)

                    @block.vector
                    def _(vector, buf=buf, i=i):
                        c_se.wait(vector)
                        vs(vector, vector.reciprocal(out=st_[:, 6:7], in_=st_[:, 5:6]))
                        vs(vector, vector.tensor_scalar(out=xn_tok[:], in0=buf[:], scalar1=st_[:, 4:5],
                                                        scalar2=st_[:, 6:7], op0=ALU.add, op1=ALU.mult))
                        vs(vector, vector.reduce_max(out=st_[:, 7:8], in_=xn_tok[:], axis=AX.X,
                                                     apply_absolute_value=True))
                        vs(vector, vector.tensor_scalar(out=cxbuf[:, i:i + 1], in0=st_[:, 7:8], scalar1=1e-5,
                                                        scalar2=None, op0=ALU.max))
                        vs(vector, vector.reciprocal(out=st_[:, 7:8], in_=cxbuf[:, i:i + 1]))
                        vs(vector, vector.tensor_scalar(out=st_[:, 7:8], in0=st_[:, 7:8], scalar1=127.0,
                                                        scalar2=None, op0=ALU.mult))
                        vs(vector, vector.tensor_scalar(out=xq_tok[:], in0=xn_tok[:], scalar1=st_[:, 7:8],
                                                        scalar2=None, op0=ALU.mult))
                        vs(vector, vector.tensor_copy(out=ci_t[:], in_=xq_tok[:]))
                        vector.scalar_tensor_tensor(out=d_t[:], in0=ci_t[:], scalar=0.5, in1=xq_tok[:],
                                                    op0=ALU.add, op1=ALU.is_le)
                        vs(vector, vector.scalar_tensor_tensor(out=e_t[:], in0=ci_t[:], scalar=-0.5, in1=xq_tok[:],
                                                               op0=ALU.add, op1=ALU.is_ge))
                        vs(vector, vector.tensor_sub(out=e_t[:], in0=e_t[:], in1=d_t[:]))
                        v = vector.scalar_tensor_tensor(out=xq_tok[:], in0=ci_t[:], scalar=-1.0,
                                                        in1=e_t[:], op0=ALU.mult, op1=ALU.add)
                        c_ve.inc(v)

                    @block.tensor
                    def _(tensor):
                        c_ve.wait(tensor)
                        for j in range(NEC):
                            tensor.transpose(PS[:, j * 128:(j + 1) * 128], xn_tok[:, j * 128:(j + 1) * 128], ident[:])
                        for j in range(NEC):
                            m = tensor.transpose(PS[:, 1024 + j * 128:1024 + (j + 1) * 128],
                                                 xq_tok[:, j * 128:(j + 1) * 128], ident[:])
                        c_te.inc(m)

                    @block.vector
                    def _(vector, a=a):
                        c_te.wait(vector)
                        vector.wait_ge(c_wr.sem, c_wr.n)
                        vector.tensor_copy(
                            out=xn_fm[:, :].rearrange("p (j t) -> p j t", t=T)[:, :, a:a + 128],
                            in_=PS[:, 0:1024].rearrange("p (j t) -> p j t", t=128))
                        v = vector.tensor_copy(
                            out=xqfm_sb[:, :],
                            in_=PS[:, 1024:2048].rearrange("p (j t) -> p j t", t=128))
                        c_ve.inc(v)

                    @block.gpsimd
                    def _(gpsimd, a=a):
                        c_ve.wait(gpsimd)
                        g = gpsimd.dma_start(out=xq_dram.ap()[:, :, a:a + 128],
                                             in_=xqfm_sb[:, :].rearrange("p (j t) -> p j t", t=128))
                        c_wr.inc(g, 16)

                barrier()

            with ExitStack() as ph:
                wsb = ph.enter_context(nc.sbuf_tensor(f"wsb{s}", [128, NEC * E], F32))
                stg = ph.enter_context(nc.sbuf_tensor(f"stg{s}", [128, 1024], F32))
                stg2 = ph.enter_context(nc.sbuf_tensor(f"stg2{s}", [128, 2048], F32))

                for wi, (w_in, w_dr) in enumerate([(wq_in, q_dram), (wk_in, k_dram)]):
                    @block.sync
                    def _(sync, w_in=w_in):
                        for ec in range(NEC):
                            c_ld.inc(sync.dma_start(out=wsb[:, ec * E:(ec + 1) * E],
                                                    in_=w_in.ap()[ec * 128:(ec + 1) * 128, :]), 16)

                    for dc in range(NEC):
                        @block.tensor
                        def _(tensor, dc=dc):
                            c_ld.wait(tensor)
                            if dc > 0:
                                tensor.wait_ge(c_ve.sem, c_ve.n)
                            for ec in range(NEC):
                                for fs in range(4):
                                    m = tensor.matmul(PS[:, fs * 512:(fs + 1) * 512],
                                                      wsb[:, ec * E + dc * 128: ec * E + (dc + 1) * 128],
                                                      xn_fm[:, ec * T + fs * 512: ec * T + (fs + 1) * 512],
                                                      start=(ec == 0), stop=(ec == NEC - 1))
                            c_te.inc(m)

                        @block.vector
                        def _(vector, dc=dc):
                            c_te.wait(vector)
                            vector.wait_ge(c_wr.sem, c_wr.n)
                            v = vector.tensor_copy(out=stg2[:, :], in_=PS[:, 0:2048])
                            c_ve.inc(v)

                        @block.gpsimd
                        def _(gpsimd, w_dr=w_dr, dc=dc):
                            c_ve.wait(gpsimd)
                            c_wr.inc(gpsimd.dma_start(out=w_dr.ap()[:, dc, :], in_=stg2[:, :]), 16)
                    barrier()

                @block.sync
                def _(sync):
                    for ec in range(NEC):
                        c_ld.inc(sync.dma_start(out=wsb[:, ec * E:(ec + 1) * E],
                                                in_=wv_in.ap()[ec * 128:(ec + 1) * 128, :]), 16)

                for i in range(NT):
                    a = i * 128

                    @block.tensor
                    def _(tensor, a=a, i=i):
                        c_ld.wait(tensor)
                        if i > 0:
                            tensor.wait_ge(c_ve.sem, c_ve.n)
                        po = (i % 2) * 1024
                        for ec in range(NEC):
                            for vf in range(2):
                                m = tensor.matmul(PS[:, po + vf * 512:po + (vf + 1) * 512],
                                                  xn_fm[:, ec * T + a: ec * T + a + 128],
                                                  wsb[:, ec * E + vf * 512: ec * E + (vf + 1) * 512],
                                                  start=(ec == 0), stop=(ec == NEC - 1))
                        c_te.inc(m)

                    @block.vector
                    def _(vector, i=i):
                        c_te.wait(vector)
                        vector.wait_ge(c_wr.sem, c_wr.n)
                        po = (i % 2) * 1024
                        v = vector.tensor_copy(out=stg[:, 0:1024], in_=PS[:, po:po + 1024])
                        c_ve.inc(v)

                    @block.gpsimd
                    def _(gpsimd, a=a):
                        c_ve.wait(gpsimd)
                        c_wr.inc(gpsimd.dma_start(out=v_dram.ap()[a:a + 128, :], in_=stg[:, :]), 16)
                barrier()

            sub_ab.close()

            scope_cf = ExitStack()
            if True:
                ph = scope_cf
                qring = [ph.enter_context(nc.sbuf_tensor(f"qr{s}{i}", [128, NEC * 128], F32)) for i in range(2)]
                rstage = ph.enter_context(nc.sbuf_tensor(f"rst{s}", [128, 8 * 512], F32))
                qrt16 = ph.enter_context(nc.sbuf_tensor(f"qrt16{s}", [128, H * 128], F32))

                @block.vector
                def _(vector):
                    for hh in range(7):
                        vector.memset(rstage[:, hh * 512 + 257: (hh + 1) * 512], NEG)
                    vector.memset(rstage[:, 7 * 512 + 257: 8 * 512], NEG).then_inc(c_ve.sem, 1)
                c_ve.n += 1

                for i in range(NT):
                    a = i * 128
                    qb = qring[i % 2]

                    @block.sync
                    def _(sync, qb=qb, a=a, i=i):
                        if i >= 2:
                            sync.wait_ge(c_te.sem, c_te.n)
                        sync.wait_ge(c_wr.sem, c_wr.n)
                        r = sync.dma_start(out=qb[:, :].rearrange("p (j t) -> p j t", t=128),
                                           in_=q_dram.ap()[:, :, a:a + 128])
                        c_qr[i % 2].inc(r, 16)

                    @block.vector
                    def _(vector, qb=qb, i=i):
                        c_qr[i % 2].wait(vector)
                        c_rd.wait(vector)
                        vector.wait_ge(c_te.sem, c_te.n)
                        last = None
                        for h in range(H):
                            off = (h % 2) * 64
                            jj = h // 2
                            last = vector.tensor_scalar(out=qrt16[off:off + 64, h * 128:(h + 1) * 128],
                                                        in0=qb[off:off + 64, jj * 128:(jj + 1) * 128],
                                                        scalar1=rb_sb[off:off + 64, h:h + 1],
                                                        scalar2=SC, op0=ALU.add, op1=ALU.mult)
                        c_ve.inc(last)

                    for g in range(2):
                        for q4 in range(4):
                            @block.tensor
                            def _(tensor, g=g, q4=q4, i=i):
                                c_ve.wait(tensor)
                                for k in range(2):
                                    hh = q4 * 2 + k
                                    h = g * 8 + hh
                                    off = (h % 2) * 64
                                    m = tensor.matmul(PS[:, 2048 + k * 512: 2048 + k * 512 + 257],
                                                      qrt16[off:off + 64, h * 128:(h + 1) * 128],
                                                      pt_sb[off:off + 64, :],
                                                      start=True, stop=True)
                                c_te.inc(m)

                            @block.vector
                            def _(vector, g=g, q4=q4, i=i):
                                c_te.wait(vector)
                                if q4 == 0:
                                    vector.wait_ge(c_rho.sem, c_rho.n)
                                if i == 0 or i == NT - 1:
                                    edge = edge0_sb if i == 0 else edge15_sb
                                    last = None
                                    for k in range(2):
                                        hh = q4 * 2 + k
                                        last = vector.tensor_add(out=rstage[:, hh * 512:hh * 512 + 257],
                                                                 in0=PS[:, 2048 + k * 512:2048 + k * 512 + 257],
                                                                 in1=edge[:, :])
                                    c_ve.inc(last)
                                else:
                                    v = vector.tensor_copy(
                                        out=rstage[:, :].rearrange("p (h c) -> p h c", c=512)[:, q4 * 2:(q4 + 1) * 2, 0:257],
                                        in_=PS[:, 2048:].rearrange("p (b c) -> p b c", c=512)[:, 0:2, 0:257])
                                    c_ve.inc(v)

                        @block.gpsimd
                        def _(gpsimd, g=g, a=a):
                            c_ve.wait(gpsimd)
                            w = gpsimd.dma_start(
                                out=bass.AP(rho_dram, (g * 8) * T * 512 + a * 512,
                                            [[512, 128], [T * 512, 8], [1, 512]]),
                                in_=rstage[:, :].rearrange("p (h c) -> p h c", c=512))
                            c_rho.inc(w, 16)

            with ExitStack() as ph:
                w1q_sb = ph.enter_context(nc.sbuf_tensor(f"w1q{s}", [128, NEC * DH], BF16))
                xqr = [ph.enter_context(nc.sbuf_tensor(f"xqr{s}{i}", [128, NEC * 258], BF16)) for i in range(2)]
                hcb = [ph.enter_context(nc.sbuf_tensor(f"hcb{s}{i}", [128, 258], F32)) for i in range(2)]
                ctm = [ph.enter_context(nc.sbuf_tensor(f"ctm{s}{i}", [128, 256], F32)) for i in range(2)]
                hss = ph.enter_context(nc.sbuf_tensor(f"hss{s}", [128, NCC * 256], F32))
                qy = ph.enter_context(nc.sbuf_tensor(f"qy{s}", [128, 1024], F32))
                qc = ph.enter_context(nc.sbuf_tensor(f"qc{s}", [128, 1024], I32))
                qd = ph.enter_context(nc.sbuf_tensor(f"qd{s}", [128, 1024], F32))
                qe = ph.enter_context(nc.sbuf_tensor(f"qe{s}", [128, 1024], F32))
                x2t = ph.enter_context(nc.sbuf_tensor(f"x2t{s}", [128, 1024], F32))
                x2fm = ph.enter_context(nc.sbuf_tensor(f"x2fm{s}", [128, NCC * 128], BF16))
                abc = ph.enter_context(nc.sbuf_tensor(f"abc{s}", [128, 258], F32))
                mx4 = ph.enter_context(nc.sbuf_tensor(f"mx4{s}", [128, 8], F32))
                cxs = ph.enter_context(nc.sbuf_tensor(f"cxs{s}", [128, NT], F32))
                cxT = ph.enter_context(nc.sbuf_tensor(f"cxT{s}", [NT, 128], F32))

                @block.sync
                def _(sync):
                    sync.wait_ge(c_wr.sem, c_wr.n)
                    for ec in range(NEC):
                        c_ld.inc(sync.dma_start(out=w1q_sb[:, :].rearrange("p (j h) -> p j h", h=DH)[:, ec, :],
                                                in_=w1q_dram.ap()[:, ec, :]), 16)

                @block.vector
                def _(vector):
                    v = vector.tensor_scalar(out=cxs[:, :], in0=cxbuf[:, :], scalar1=sca[:, 1:2],
                                             scalar2=None, op0=ALU.mult)
                    c_ve.inc(v)

                @block.tensor
                def _(tensor):
                    c_ve.wait(tensor)
                    c_ld.wait(tensor)
                    m = tensor.transpose(PS[0:NT, 0:128], cxs[:, 0:NT], ident[:])
                    c_te.inc(m)

                @block.vector
                def _(vector):
                    c_te.wait(vector)
                    vector.tensor_copy(out=cxT[:, :], in_=PS[0:NT, 0:128])
                    vector.memset(arow[0:1, 0:1], 0.0)
                    v = vector.memset(arow[0:1, T + 1:T + 2], 0.0)
                    c_ve.inc(v)

                @block.gpsimd
                def _(gpsimd):
                    c_ve.wait(gpsimd)
                    g = gpsimd.dma_start(out=arow[0:1, 1:T + 1].rearrange("p (i t) -> p i t", t=128),
                                         in_=cxT[:, :])
                    c_rd.inc(g, 16)

                NSU = T // 256
                for u in range(NSU):
                    a = u * 256
                    xb = xqr[u % 2]

                    @block.sync
                    def _(sync, xb=xb, a=a, u=u):
                        if u >= 2:
                            sync.wait_ge(c_te.sem, c_te.n)
                        lo = max(a - 1, 0)
                        hi = min(a + 257, T)
                        dst_off = lo - (a - 1)
                        r = sync.dma_start(
                            out=xb[:, :].rearrange("p (j t) -> p j t", t=258)[:, :, dst_off:dst_off + hi - lo],
                            in_=xq_dram.ap()[:, :, lo:hi])
                        c_f[u % 2].inc(r, 16)

                    @block.vector
                    def _(vector, xb=xb, u=u):
                        if u == 0:
                            vector.memset(xb[:, :].rearrange("p (j t) -> p j t", t=258)[:, :, 0:1], 0.0)
                        if u == NSU - 1:
                            vector.memset(xb[:, :].rearrange("p (j t) -> p j t", t=258)[:, :, 257:258], 0.0)
                        vector.memset(barv[0:1, 4:5], 0.0).then_inc(c_ve.sem, 1)
                    c_ve.n += 1

                    @block.tensor
                    def _(tensor, a=a, u=u):
                        c_ve.wait(tensor)
                        c_rd.wait(tensor)
                        m = tensor.matmul(PS[:, 3072:3072 + 258], ones_sb[0:1, :],
                                          arow[0:1, a:a + 258], start=True, stop=True)
                        c_te.inc(m)

                    @block.vector
                    def _(vector):
                        c_te.wait(vector)
                        v = vector.tensor_copy(out=abc[:, :], in_=PS[:, 3072:3072 + 258])
                        c_ve.inc(v)

                    veS = c_ve.n
                    gpS = c_gp.n
                    teS = c_te.n

                    @block.tensor
                    def _(tensor, xb=xb, u=u, veS=veS):
                        c_f[u % 2].wait(tensor)
                        for cc in range(NCC):
                            tensor.wait_ge(c_ve.sem, veS + (4 * (cc - 2) + 1 if cc >= 2 else 0))
                            po = 1024 + (cc % 2) * 512
                            for ec in range(NEC):
                                m = tensor.matmul(PS[:, po:po + 258],
                                                  w1q_sb[:, ec * DH + cc * 128: ec * DH + (cc + 1) * 128],
                                                  xb[:, ec * 258:(ec + 1) * 258],
                                                  start=(ec == 0), stop=(ec == NEC - 1))
                            m.then_inc(c_te.sem, 1)
                    c_te.n += NCC

                    @block.vector
                    def _(vector, teS=teS, seS=None):
                        vector.wait_ge(c_se.sem, c_se.n)
                        for cc in range(NCC):
                            vector.wait_ge(c_te.sem, teS + cc + 1)
                            po = 1024 + (cc % 2) * 512
                            hb = hcb[cc % 2]
                            tb = ctm[cc % 2]
                            w3 = cc * 3
                            vs(vector, vector.tensor_mul(out=hb[:, :], in0=PS[:, po:po + 258], in1=abc[:, :]))
                            vs(vector, vector.tensor_scalar(out=tb[:, :], in0=hb[:, 0:256],
                                                            scalar1=convw_sb[:, w3:w3 + 1],
                                                            scalar2=None, op0=ALU.mult))
                            vs(vector, vector.scalar_tensor_tensor(out=tb[:, :], in0=hb[:, 1:257],
                                                                   scalar=convw_sb[:, w3 + 1:w3 + 2],
                                                                   in1=tb[:, :], op0=ALU.mult, op1=ALU.add))
                            c_ve.inc(vector.scalar_tensor_tensor(out=hss[:, cc * 256:(cc + 1) * 256], in0=hb[:, 2:258],
                                                                 scalar=convw_sb[:, w3 + 2:w3 + 3],
                                                                 in1=tb[:, :], op0=ALU.mult, op1=ALU.add))

                    @block.scalar
                    def _(scalar):
                        c_ve.wait(scalar)
                        last = None
                        for cc in range(NCC):
                            last = scalar.activation(out=hss[:, cc * 256:(cc + 1) * 256],
                                                     in_=hss[:, cc * 256:(cc + 1) * 256], func=AF.Silu,
                                                     bias=convb_sb[:, cc:cc + 1])
                        last.then_inc(c_se.sem, 1)
                    c_se.n += 1

                    for b in range(2):
                        i = u * 2 + b

                        @block.tensor
                        def _(tensor, b=b):
                            c_se.wait(tensor)
                            tensor.wait_ge(c_ve.sem, c_ve.n)
                            for qtr in range(4):
                                for j in range(8):
                                    cc = qtr * 8 + j
                                    m = tensor.transpose(PS[:, qtr * 1024 + j * 128:qtr * 1024 + (j + 1) * 128],
                                                         hss[:, cc * 256 + b * 128: cc * 256 + b * 128 + 128],
                                                         ident[:])
                            c_te.inc(m)

                        @block.vector
                        def _(vector, b=b, i=i):
                            c_te.wait(vector)
                            for qtr in range(4):
                                vs(vector, vector.reduce_max(out=mx4[:, qtr:qtr + 1], in_=PS[:, qtr * 1024:(qtr + 1) * 1024],
                                                             axis=AX.X, apply_absolute_value=True))
                            vs(vector, vector.reduce_max(out=mx4[:, 4:5], in_=mx4[:, 0:4], axis=AX.X))
                            vs(vector, vector.tensor_scalar(out=mx4[:, 5:6], in0=mx4[:, 4:5], scalar1=1e-5,
                                                            scalar2=None, op0=ALU.max))
                            vs(vector, vector.reciprocal(out=mx4[:, 6:7], in_=mx4[:, 5:6]))
                            vs(vector, vector.tensor_scalar(out=mx4[:, 6:7], in0=mx4[:, 6:7], scalar1=127.0,
                                                            scalar2=None, op0=ALU.mult))
                            v = vector.tensor_mul(out=bbuf[:, i:i + 1], in0=mx4[:, 5:6], in1=sca[:, 3:4])
                            c_ve.inc(v)

                        for qtr in range(4):
                            @block.vector
                            def _(vector, qtr=qtr):
                                vector.wait_ge(c_ve.sem, c_ve.n)
                                vs(vector, vector.tensor_scalar(out=qy[:], in0=PS[:, qtr * 1024:(qtr + 1) * 1024],
                                                                scalar1=mx4[:, 6:7],
                                                                scalar2=None, op0=ALU.mult))
                                vs(vector, vector.tensor_copy(out=qc[:], in_=qy[:]))
                                vector.scalar_tensor_tensor(out=qd[:], in0=qc[:], scalar=0.5, in1=qy[:],
                                                            op0=ALU.add, op1=ALU.is_le)
                                vs(vector, vector.scalar_tensor_tensor(out=qe[:], in0=qc[:], scalar=-0.5, in1=qy[:],
                                                                       op0=ALU.add, op1=ALU.is_ge))
                                vs(vector, vector.tensor_sub(out=qe[:], in0=qe[:], in1=qd[:]))
                                v = vector.scalar_tensor_tensor(out=x2t[:], in0=qc[:], scalar=-1.0,
                                                                in1=qe[:], op0=ALU.mult, op1=ALU.add)
                                c_ve.inc(v)

                            @block.tensor
                            def _(tensor, qtr=qtr):
                                c_ve.wait(tensor)
                                for j in range(8):
                                    m = tensor.transpose(PS[:, j * 128:(j + 1) * 128],
                                                         x2t[:, j * 128:(j + 1) * 128], ident[:])
                                c_te.inc(m)

                            @block.vector
                            def _(vector, qtr=qtr):
                                c_te.wait(vector)
                                if qtr == 0:
                                    vector.wait_ge(c_wr.sem, c_wr.n)
                                v = vector.tensor_copy(out=x2fm[:, qtr * 1024:(qtr + 1) * 1024],
                                                       in_=PS[:, 0:1024])
                                c_ve.inc(v)

                        @block.gpsimd
                        def _(gpsimd, a=a, b=b):
                            c_ve.wait(gpsimd)
                            g = gpsimd.dma_start(out=x2q_dram.ap()[:, :, a + b * 128:a + b * 128 + 128],
                                                 in_=x2fm[:, :].rearrange("p (j t) -> p j t", t=128))
                            c_wr.inc(g, 16)
                barrier()

            scope_cf.close()

            with ExitStack() as ph:
                w2q_sb = ph.enter_context(nc.sbuf_tensor(f"w2q{s}", [128, NCC * E], BF16))
                x2r = [ph.enter_context(nc.sbuf_tensor(f"x2r{s}{i}", [128, NCC * 128], BF16)) for i in range(2)]
                ystg = ph.enter_context(nc.sbuf_tensor(f"ystg{s}", [128, E], F32))

                @block.sync
                def _(sync):
                    sync.wait_ge(c_wr.sem, c_wr.n)
                    for cc2 in range(4):
                        c_ld.inc(sync.dma_start(
                            out=w2q_sb[:, :].rearrange("p (j h) -> p j h", h=E)[:, cc2 * 8:(cc2 + 1) * 8, :],
                            in_=w2q_dram.ap()[:, cc2 * 8:(cc2 + 1) * 8, :]), 16)

                for i in range(NT):
                    a = i * 128
                    xb = x2r[i % 2]

                    @block.sync
                    def _(sync, xb=xb, a=a, i=i):
                        if i >= 2:
                            sync.wait_ge(c_te.sem, c_te.n)
                        r = sync.dma_start(out=xb[:, :].rearrange("p (j t) -> p j t", t=128),
                                           in_=x2q_dram.ap()[:, :, a:a + 128])
                        c_g[i % 2].inc(r, 16)

                    @block.tensor
                    def _(tensor, xb=xb, i=i):
                        c_ld.wait(tensor)
                        c_g[i % 2].wait(tensor)
                        if i >= 2:
                            tensor.wait_ge(c_ve.sem, c_ve.n)
                        po = (i % 2) * 1024
                        for cc in range(NCC):
                            tensor.matmul(PS[:, po + 0:po + 512],
                                          xb[:, cc * 128:(cc + 1) * 128],
                                          w2q_sb[:, cc * E:cc * E + 512],
                                          start=(cc == 0), stop=(cc == NCC - 1))
                            m = tensor.matmul(PS[:, po + 512:po + 1024],
                                              xb[:, cc * 128:(cc + 1) * 128],
                                              w2q_sb[:, cc * E + 512:(cc + 1) * E],
                                              start=(cc == 0), stop=(cc == NCC - 1))
                        c_te.inc(m)

                    @block.vector
                    def _(vector, i=i):
                        c_te.wait(vector)
                        vector.wait_ge(c_wr.sem, c_wr.n)
                        po = (i % 2) * 1024
                        v = vector.tensor_scalar(out=ystg[:, :], in0=PS[:, po:po + 1024],
                                                 scalar1=bbuf[:, i:i + 1], scalar2=None, op0=ALU.mult)
                        c_ve.inc(v)

                    @block.gpsimd
                    def _(gpsimd, a=a):
                        c_ve.wait(gpsimd)
                        c_wr.inc(gpsimd.dma_start(out=y2_dram.ap()[a:a + 128, :], in_=ystg[:]), 16)
                barrier()

            with ExitStack() as ph:
                wo_sb = ph.enter_context(nc.sbuf_tensor(f"wo{s}", [128, NEC * E], F32))
                qrg = [ph.enter_context(nc.sbuf_tensor(f"qg{s}{i}", [128, NEC * 128], F32)) for i in range(2)]
                krg = [ph.enter_context(nc.sbuf_tensor(f"kg{s}{i}", [128, NEC * 128], F32)) for i in range(4)]
                vrg = [ph.enter_context(nc.sbuf_tensor(f"vg{s}{i}", [128, E], F32)) for i in range(4)]
                skw2 = [ph.enter_context(nc.sbuf_tensor(f"skw{s}{i}", [128, 8 * 384], F32)) for i in range(2)]
                ssb = ph.enter_context(nc.sbuf_tensor(f"ssb{s}", [128, 8 * 384], F32))
                psb = ph.enter_context(nc.sbuf_tensor(f"psb{s}", [128, 8 * 384], F32))
                ptb = ph.enter_context(nc.sbuf_tensor(f"ptb{s}", [128, 3072], F32))
                qt_sb = ph.enter_context(nc.sbuf_tensor(f"qt{s}", [128, 8 * 128], F32))
                mneg = ph.enter_context(nc.sbuf_tensor(f"mneg{s}", [128, 8], F32))
                zcol = ph.enter_context(nc.sbuf_tensor(f"zcol{s}", [128, 16], F32))
                attok = ph.enter_context(nc.sbuf_tensor(f"attok{s}", [128, E], F32))
                atfm = ph.enter_context(nc.sbuf_tensor(f"atfm{s}", [128, NEC * 128], F32))
                y2r = ph.enter_context(nc.sbuf_tensor(f"y2r{s}", [128, E], F32))
                xmr = ph.enter_context(nc.sbuf_tensor(f"xmr{s}", [128, E], F32))
                otok = ph.enter_context(nc.sbuf_tensor(f"otok{s}", [128, E], F32))

                @block.sync
                def _(sync):
                    sync.wait_ge(c_wr.sem, c_wr.n)
                    for ec in range(NEC):
                        c_ld.inc(sync.dma_start(out=wo_sb[:, ec * E:(ec + 1) * E],
                                                in_=wo_in.ap()[ec * 128:(ec + 1) * 128, :]), 16)

                @block.vector
                def _(vector):
                    vector.memset(krg[3][:, :], 0.0)
                    vector.memset(vrg[3][:, :], 0.0).then_inc(c_ve.sem, 1)
                c_ve.n += 1

                skw_con = []
                qtn = [0, 0]; ctn = [0, 0]; redn = [0, 0]; expn = [0, 0]; ttn = [0, 0]; evn = [0, 0]

                def kv_slot(ti):
                    return 3 if (ti < 0 or ti >= NT) else ti % 3

                @block.sync
                def _(sync):
                    for ti in (0, 1):
                        c_kv.inc(sync.dma_start(out=krg[ti % 3][:, :].rearrange("p (j t) -> p j t", t=128),
                                                in_=k_dram.ap()[:, :, ti * 128:(ti + 1) * 128]), 16)
                        c_kv.inc(sync.dma_start(out=vrg[ti % 3][:, :], in_=v_dram.ap()[ti * 128:(ti + 1) * 128, :]), 16)

                for i in range(NT):
                    a = i * 128
                    qb = qrg[i % 2]

                    @block.sync
                    def _(sync, qb=qb, a=a, i=i):
                        if i >= 2:
                            sync.wait_ge(c_te.sem, c_te.n)
                        if i >= 1:
                            sync.wait_ge(c_ve.sem, c_ve.n)
                        c_qr[i % 2].inc(sync.dma_start(out=qb[:, :].rearrange("p (j t) -> p j t", t=128),
                                                       in_=q_dram.ap()[:, :, a:a + 128]), 16)
                        if i + 1 < NT and i > 0:
                            ti = i + 1
                            c_kv.inc(sync.dma_start(out=krg[ti % 3][:, :].rearrange("p (j t) -> p j t", t=128),
                                                    in_=k_dram.ap()[:, :, ti * 128:(ti + 1) * 128]), 16)
                            c_kv.inc(sync.dma_start(out=vrg[ti % 3][:, :],
                                                    in_=v_dram.ap()[ti * 128:(ti + 1) * 128, :]), 16)
                        c_mrg.inc(sync.dma_start(out=xmr[:], in_=x_in.ap()[s, a:a + 128, :]), 16)
                        c_mrg.inc(sync.dma_start(out=y2r[:], in_=y2_dram.ap()[a:a + 128, :]), 16)

                    for g in range(2):
                        gi = i * 2 + g
                        skw = skw2[gi % 2]

                        @block.gpsimd
                        def _(gpsimd, g=g, a=a, i=i, gi=gi, vtgt=None):
                            if gi >= 2:
                                gpsimd.wait_ge(c_te.sem, skw_con[gi - 2])
                            r = gpsimd.dma_start(
                                out=skw[:, :].rearrange("p (h u) -> p h u", u=384),
                                in_=bass.AP(rho_dram, (g * 8) * T * 512 + a * 512,
                                            [[511, 128], [T * 512, 8], [1, 384]]))
                            c_skw2[gi % 2].inc(r, 16)

                        for X in range(2):
                            hlist = (0, 2, 1, 3) if X == 0 else (4, 6, 5, 7)
                            pb = X * 1536

                            @block.vector
                            def _(vector, qb=qb, g=g, i=i, X=X, hlist=hlist, cw=ctn[X]):
                                c_qr[i % 2].wait(vector)
                                c_kv.wait(vector)
                                if cw:
                                    vector.wait_ge(c_te.sem, cw)
                                last = None
                                for hh in hlist:
                                    h = g * 8 + hh
                                    off = (h % 2) * 64
                                    jj = h // 2
                                    last = vector.tensor_scalar(out=qt_sb[off:off + 64, hh * 128:(hh + 1) * 128],
                                                                in0=qb[off:off + 64, jj * 128:(jj + 1) * 128],
                                                                scalar1=cb_sb[off:off + 64, h:h + 1],
                                                                scalar2=SC, op0=ALU.add, op1=ALU.mult)
                                c_ve.inc(last)
                            qtn[X] = c_ve.n

                            @block.tensor
                            def _(tensor, g=g, i=i, gi=gi, skw=skw, X=X, pb=pb, hlist=hlist,
                                  first=(g == 0), qw=qtn[X], ew=evn[X], sw=expn[X]):
                                if first and X == 0:
                                    c_ve.wait(tensor)
                                else:
                                    tensor.wait_ge(c_ve.sem, qw)
                                    if ew:
                                        tensor.wait_ge(c_ve.sem, ew)
                                if sw:
                                    tensor.wait_ge(c_se.sem, sw)
                                if X == 0:
                                    c_skw2[gi % 2].wait(tensor)
                                for c in range(3):
                                    tensor.matmul(PS[:, pb + c * 512:pb + (c + 1) * 512], ident[:],
                                                  skw[:, pb + c * 512:pb + (c + 1) * 512], start=True, stop=False)
                                for hh in hlist:
                                    h = g * 8 + hh
                                    off = (h % 2) * 64
                                    jj = h // 2
                                    for ci, ti in enumerate((i - 1, i, i + 1)):
                                        kb = krg[kv_slot(ti)]
                                        m = tensor.matmul(PS[:, hh * 384 + ci * 128: hh * 384 + (ci + 1) * 128],
                                                          qt_sb[off:off + 64, hh * 128:(hh + 1) * 128],
                                                          kb[off:off + 64, jj * 128:(jj + 1) * 128],
                                                          start=False, stop=True)
                                c_te.inc(m)
                            ctn[X] = c_te.n

                            @block.vector
                            def _(vector, X=X, cw=ctn[X]):
                                vector.wait_ge(c_te.sem, cw)
                                last = None
                                for hh in range(X * 4, X * 4 + 4):
                                    last = vector.reduce_max(out=mneg[:, hh:hh + 1],
                                                             in_=PS[:, hh * 384:(hh + 1) * 384],
                                                             axis=AX.X, negate=True)
                                c_ve.inc(last)
                            redn[X] = c_ve.n

                            @block.scalar
                            def _(scalar, g=g, X=X, rw=redn[X], tw=ttn[X]):
                                scalar.wait_ge(c_ve.sem, rw)
                                if tw:
                                    scalar.wait_ge(c_te.sem, tw)
                                last = None
                                for hh in range(X * 4, X * 4 + 4):
                                    h = g * 8 + hh
                                    last = scalar.activation(out=psb[:, hh * 384:(hh + 1) * 384],
                                                             in_=PS[:, hh * 384:(hh + 1) * 384],
                                                             func=AF.Exp, bias=mneg[:, hh:hh + 1],
                                                             accum_out=zcol[:, h:h + 1])
                                last.then_inc(c_se.sem, 1)
                            c_se.n += 1
                            expn[X] = c_se.n
                        skw_con.append(ctn[1])

                        for X in range(2):
                            @block.tensor
                            def _(tensor, X=X, sw=expn[X], rw=redn[X]):
                                tensor.wait_ge(c_se.sem, sw)
                                tensor.wait_ge(c_ve.sem, rw)
                                for k in range(X * 12, X * 12 + 12):
                                    m = tensor.transpose(PS[:, k * 128:(k + 1) * 128],
                                                         psb[:, k * 128:(k + 1) * 128], ident[:])
                                c_te.inc(m)
                            ttn[X] = c_te.n

                            @block.vector
                            def _(vector, X=X, tw=ttn[X]):
                                vector.wait_ge(c_te.sem, tw)
                                v = vector.tensor_copy(out=ptb[:, X * 1536:(X + 1) * 1536],
                                                       in_=PS[:, X * 1536:(X + 1) * 1536])
                                c_ve.inc(v)
                            evn[X] = c_ve.n

                        for X in range(2):
                            @block.tensor
                            def _(tensor, g=g, i=i, X=X, ew=evn[X]):
                                tensor.wait_ge(c_ve.sem, ew)
                                for hh in range(X * 4, X * 4 + 4):
                                    h = g * 8 + hh
                                    for ci, ti in enumerate((i - 1, i, i + 1)):
                                        vb = vrg[kv_slot(ti)]
                                        m = tensor.matmul(PS[:, 3072 + hh * 64: 3072 + (hh + 1) * 64],
                                                          ptb[:, hh * 384 + ci * 128:hh * 384 + (ci + 1) * 128],
                                                          vb[:, h * 64:(h + 1) * 64],
                                                          start=(ci == 0), stop=(ci == 2))
                                c_te.inc(m)

                        @block.vector
                        def _(vector, g=g):
                            c_te.wait(vector)
                            last = None
                            for hh in range(8):
                                h = g * 8 + hh
                                last = vector.reciprocal(out=zcol[:, h:h + 1], in_=zcol[:, h:h + 1])
                            vs(vector, last)
                            last = None
                            for hh in range(8):
                                h = g * 8 + hh
                                last = vector.tensor_scalar(out=attok[:, h * 64:(h + 1) * 64],
                                                            in0=PS[:, 3072 + hh * 64:3072 + (hh + 1) * 64],
                                                            scalar1=zcol[:, h:h + 1], scalar2=None, op0=ALU.mult)
                            last.then_inc(c_ve.sem, 1)
                        c_ve.n += 1

                    @block.tensor
                    def _(tensor):
                        c_ve.wait(tensor)
                        for j in range(NEC):
                            m = tensor.transpose(PS[:, j * 128:(j + 1) * 128],
                                                 attok[:, j * 128:(j + 1) * 128], ident[:])
                        c_te.inc(m)

                    @block.vector
                    def _(vector):
                        c_te.wait(vector)
                        v = vector.tensor_copy(out=atfm[:], in_=PS[:, 0:1024])
                        c_ve.inc(v)

                    @block.tensor
                    def _(tensor, i=i):
                        c_ve.wait(tensor)
                        c_ld.wait(tensor)
                        for ec in range(NEC):
                            for vf in range(2):
                                m = tensor.matmul(PS[:, 1024 + vf * 512:1024 + (vf + 1) * 512],
                                                  atfm[:, ec * 128:(ec + 1) * 128],
                                                  wo_sb[:, ec * E + vf * 512:ec * E + (vf + 1) * 512],
                                                  start=(ec == 0), stop=(ec == NEC - 1))
                        c_te.inc(m)

                    @block.vector
                    def _(vector, i=i):
                        c_te.wait(vector)
                        c_mrg.wait(vector)
                        vector.wait_ge(c_out.sem, c_out.n)
                        vs(vector, vector.tensor_add(out=otok[:, :], in0=PS[:, 1024:2048], in1=bo_b[:, :]))
                        vs(vector, vector.tensor_add(out=otok[:, :], in0=otok[:, :], in1=xmr[:, :]))
                        v = vector.tensor_add(out=otok[:, :], in0=otok[:, :], in1=y2r[:, :])
                        c_ve.inc(v)

                    @block.gpsimd
                    def _(gpsimd, a=a, i=i):
                        c_ve.wait(gpsimd)
                        c_out.inc(gpsimd.dma_start(out=out_dr.ap()[s, a:a + 128, :], in_=otok[:]), 16)
                barrier()

    @block.gpsimd
    def _(gpsimd):
        gpsimd.wait_ge(c_out.sem, c_out.n)

    stk.close()
    return nc


_ENC = None


def _make_consts():
    global _ENC
    import jax
    import jax.numpy as jnp
    if _ENC is None:
        cpu = jax.local_devices(backend="cpu")[0]
        with jax.default_device(cpu):
            pos = jnp.arange(-T + 1, T, dtype=jnp.float32)
            freqs = jnp.arange(0, D, 2, dtype=jnp.float32)
            inv = 1.0 / 10000.0 ** (freqs / D)
            pe = pos[:, None] * inv[None, :]
            _ENC = np.asarray(jnp.concatenate([jnp.sin(pe), jnp.cos(pe)], axis=-1))
    enc = _ENC
    sel = enc[2175 - np.arange(257), :].T.astype(np.float32)
    encsel2 = np.concatenate([sel, sel], axis=0)
    ident = np.eye(128, dtype=np.float32)
    t_idx = np.arange(128)[:, None]
    m_idx = np.arange(257)[None, :]
    edge0 = np.where(m_idx < 128 - t_idx, np.float32(NEG), np.float32(0.0)).astype(np.float32)
    edge15 = np.where(m_idx >= 256 - t_idx, np.float32(NEG), np.float32(0.0)).astype(np.float32)
    return encsel2, ident, edge0, edge15


def kernel(L0_attn_x, L0_attn_W_q, L0_attn_W_k, L0_attn_W_v, L0_attn_content_bias,
           L0_attn_relative_bias, L0_attn_W_rel, L0_attn_b_rel, L0_attn_W_o, L0_attn_b_o,
           L0_ffn_w1, L0_ffn_conv_w, L0_ffn_conv_b, L0_ffn_w2):
    x = np.ascontiguousarray(np.asarray(L0_attn_x, dtype=np.float32))
    encsel2, ident, edge0, edge15 = _make_consts()

    cbT = np.asarray(L0_attn_content_bias, np.float32).T
    rbT = np.asarray(L0_attn_relative_bias, np.float32).T
    cbT2 = np.concatenate([cbT, cbT], axis=0)
    rbT2 = np.concatenate([rbT, rbT], axis=0)
    wrel = np.asarray(L0_attn_W_rel, np.float32)
    wrel2 = np.concatenate([wrel, wrel], axis=0)
    brel = np.asarray(L0_attn_b_rel, np.float32).reshape(64, 1)
    brel2 = np.concatenate([brel, brel], axis=0)
    bo = np.asarray(L0_attn_b_o, np.float32).reshape(1, E)
    w1t = np.ascontiguousarray(np.asarray(L0_ffn_w1, np.float32).T)
    w2t = np.ascontiguousarray(np.asarray(L0_ffn_w2, np.float32).T)
    cw = np.asarray(L0_ffn_conv_w, np.float32).reshape(DH, 3)
    convw = np.ascontiguousarray(cw.reshape(NCC, 128, 3).transpose(1, 0, 2).reshape(128, NCC * 3))
    convb = np.ascontiguousarray(np.asarray(L0_ffn_conv_b, np.float32).reshape(NCC, 128).T)

    common = {
        "wq": np.ascontiguousarray(np.asarray(L0_attn_W_q, np.float32)),
        "wk": np.ascontiguousarray(np.asarray(L0_attn_W_k, np.float32)),
        "wv": np.ascontiguousarray(np.asarray(L0_attn_W_v, np.float32)),
        "wo": np.ascontiguousarray(np.asarray(L0_attn_W_o, np.float32)),
        "cbT2": np.ascontiguousarray(cbT2), "rbT2": np.ascontiguousarray(rbT2),
        "wrel2": np.ascontiguousarray(wrel2), "brel2": np.ascontiguousarray(brel2),
        "bo": np.ascontiguousarray(bo),
        "w1t": w1t, "w2t": w2t, "convw": convw, "convb": convb,
        "encsel2": np.ascontiguousarray(encsel2), "ident": ident,
        "edge0": np.ascontiguousarray(edge0), "edge15": np.ascontiguousarray(edge15),
    }
    nc = build_nc()
    in_maps = []
    for c in range(8):
        m = dict(common)
        m["x"] = np.ascontiguousarray(x[2 * c:2 * c + 2])
        in_maps.append(m)
    res = run_bass_kernel_spmd(nc, in_maps, core_ids=list(range(8)))
    out = np.concatenate([r["out"].reshape(NSEQ, T, E) for r in res.results], axis=0)
    return out.astype(np.float32)
